# version 11
# baseline (speedup 1.0000x reference)
import numpy as np
import ml_dtypes
from contextlib import ExitStack

import concourse.bass as bass
import concourse.mybir as mybir
from concourse.bass_utils import run_bass_kernel_spmd

F32 = mybir.dt.float32
BF16 = mybir.dt.bfloat16
AF = mybir.ActivationFunctionType
ALU = mybir.AluOpType
AX = mybir.AxisListType

D = 1024
T = 2048
NT = 16
NDT = 8
H = 8
DH = 128
FF = 4096
NFT = 32
NE = 8
K0 = 4.0
EPS = 1e-5
NCH = 4


def build_program(depth=6, do_moe=True, dbg=(), use_lrelu=True, stop_stage=None):
    nc = bass.Bass("TRN2", target_bir_lowering=False)

    x_in = nc.dram_tensor("x_in", [T, D], F32, kind="ExternalInput")
    wq_d = nc.dram_tensor("wq", [6, D, D], F32, kind="ExternalInput")
    wk_d = nc.dram_tensor("wk", [6, D, D], F32, kind="ExternalInput")
    wv_d = nc.dram_tensor("wv", [6, D, D], F32, kind="ExternalInput")
    wo_d = nc.dram_tensor("wo", [6, D, D], F32, kind="ExternalInput")
    w1_d = nc.dram_tensor("w1", [6, D, FF], F32, kind="ExternalInput")
    w2_d = nc.dram_tensor("w2", [6, FF, D], F32, kind="ExternalInput")
    ln1w_d = nc.dram_tensor("ln1w", [6, D], F32, kind="ExternalInput")
    ln1b_d = nc.dram_tensor("ln1b", [6, D], F32, kind="ExternalInput")
    ln2w_d = nc.dram_tensor("ln2w", [6, D], F32, kind="ExternalInput")
    ln2b_d = nc.dram_tensor("ln2b", [6, D], F32, kind="ExternalInput")
    bo_d = nc.dram_tensor("bo", [6, D], F32, kind="ExternalInput")
    wr_d = nc.dram_tensor("wr", [D, NE], F32, kind="ExternalInput")
    br_d = nc.dram_tensor("br", [1, NE], F32, kind="ExternalInput")
    W1m_d = nc.dram_tensor("W1m", [NE, D, FF], F32, kind="ExternalInput")
    b1m_d = nc.dram_tensor("b1m", [NE, FF], F32, kind="ExternalInput")
    W2m_d = nc.dram_tensor("W2m", [NE, FF, D], F32, kind="ExternalInput")
    b2m_d = nc.dram_tensor("b2m", [NE, D], F32, kind="ExternalInput")
    ident_d = nc.dram_tensor("ident", [128, 128], F32, kind="ExternalInput")
    out_d = nc.dram_tensor("out", [T, D], F32, kind="ExternalOutput")
    cc_in = nc.dram_tensor("cc_in", [128, 1032], F32)
    cc_out = nc.dram_tensor("cc_out", [128, 1032], F32)
    xn0_dram = nc.dram_tensor("xn0_dram", [128, NDT * T], BF16)
    s_dram = nc.dram_tensor("s_dram", [T, D], F32)
    dbg_t = {}
    for name in dbg:
        shp = [128, NT * NE] if name == "wrt" else [128, NDT * T]
        dbg_t[name] = nc.dram_tensor("dbg_" + name, shp, F32, kind="ExternalOutput")
    stop_d = None
    if stop_stage is not None:
        stop_d = nc.dram_tensor("stopdump", [128, NDT * T], F32, kind="ExternalOutput")

    class StopEmit(Exception):
        pass

    with ExitStack() as ctx:
        sb = lambda n, s, d: ctx.enter_context(nc.sbuf_tensor(n, s, d))
        xa = sb("xa", [128, NDT * T], F32)
        xnT = sb("xnT", [128, NDT * T], BF16)
        eqT = sb("eqT", [128, H * T], BF16)
        hs = [sb("hs%d" % i, [128, 4096], BF16) for i in range(4)]
        hs4 = sb("hs4", [128, 8192], BF16)
        ek = [sb("ek%d" % i, [128, D], BF16) for i in range(2)]
        vt = [sb("vt%d" % i, [128, H * 129], BF16) for i in range(2)]
        b2row = vt[0][0:1, 0:D]
        spkg = sb("spkg", [128, 1032], F32)
        ctxbf = sb("ctxbf", [128, H * DH], BF16)
        aps1 = sb("aps1", [128, 512], F32)
        aps2 = sb("aps2", [128, 512], F32)
        actx = aps2
        arow = sb("arow", [1, 512], F32)
        mrow_f = sb("mrow_f", [1, 512], F32)
        mrow_b = sb("mrow_b", [1, 512], BF16)
        rrow_b = sb("rrow_b", [1, 512], BF16)
        vrow = sb("vrow", [1, 512], F32)
        vrow2 = None
        rqrow = None
        lnrows = hs4[:, :].bitcast(F32)[0:32, 0:D]
        lncols = sb("lncols", [128, NDT * 32], F32)
        b1c = sb("b1c", [128, NFT * NE], F32)
        brrow = sb("brrow", [1, NE], BF16)
        wrb = sb("wrb", [128, NDT * NE], BF16)
        wrt = sb("wrt", [128, NT * NE], F32)
        lgt = sb("lgt", [128, NT * NE], F32)
        rcol = [sb("rc%d" % i, [128, 1], F32) for i in range(4)]
        onesrow = sb("onesrow", [1, 512], BF16)
        onescol = sb("onescol", [128, 1], BF16)
        negk0c = sb("negk0c", [128, 1], F32)
        epscol = sb("epscol", [128, 1], F32)
        identf = sb("identf", [128, 128], F32)
        eqf = eqT[:, :].bitcast(F32)
        x0b = [eqf[:, i * D:(i + 1) * D] for i in range(2)]
        stb = [eqf[:, (2 + i) * D:(3 + i) * D] for i in range(2)]
        gbuf = eqf[:, 4 * D:5 * D]
        mf_B = eqf[0:1, 5120:5632]
        v1_B = eqf[0:1, 5632:6144]
        m2_B = eqf[0:1, 6144:6656]
        mb_B = eqT[0:1, 13312:13824]
        rr_B = eqT[0:1, 13824:14336]
        xsq = [eqT[:, 14336:14848], eqT[:, 14848:15360]]
        m2_A = None
        ps = [ctx.enter_context(nc.psum_tensor("ps%d" % i, [128, 512], F32))
              for i in range(8)]

        semh, cnt = {}, {}
        for s in ["MM", "VV", "SS", "GG", "CC", "W0", "W1", "W2", "W3",
                  "W4", "W5", "GC", "CD", "B2", "YC", "F0", "F1", "SP", "OUT"]:
            semh[s] = ctx.enter_context(nc.semaphore(s))
            cnt[s] = 0
        streams = {k: [] for k in "tvsgy"}

        def E(eng, fn):
            streams[eng].append(fn)

        def EI(eng, fn, sem, k=1):
            cnt[sem] += k
            E(eng, lambda e, fn=fn, s=semh[sem], k=k: fn(e).then_inc(s, k))
            return cnt[sem]

        def EW(eng, sem, v):
            if v > 0:
                E(eng, lambda e, s=semh[sem], v=v: e.wait_ge(s, v))

        def VW(fn):
            v = EI("v", fn, "VV", 1)
            EW("v", "VV", v)
            return v

        ps_w = [("MM", 0)] * 8
        ps_r = [[] for _ in range(8)]

        def psW(i, eng="t"):
            for s, v in ps_r[i]:
                EW(eng, s, v)
            ps_r[i] = []

        def psMW(i, val):
            ps_w[i] = ("MM", val)

        def psR(i, eng):
            s, v = ps_w[i]
            EW(eng, s, v)

        def psMR(i, sem, val):
            ps_r[i].append((sem, val))

        def load_mat_1024(slots, sems, src3, l, wait_mm=0):
            if wait_mm:
                EW("g", "MM", wait_mm)
            rdy = []
            for j in range(2):
                src = src3[l, j * 512:(j + 1) * 512, :].rearrange(
                    "(kt p) m -> p kt m", p=128)
                sl = slots[j] if isinstance(slots[j], bass.AP) else slots[j][:, :]
                dst = sl.rearrange("p (kt m) -> p kt m", m=1024)
                v = EI("g", lambda e, d=dst, s=src:
                       e.dma_start(out=d, in_=s), sems[j], 16)
                rdy.append((sems[j], v))
            return rdy

        def EWL(eng, lst):
            for s_, v_ in lst:
                EW(eng, s_, v_)

        def maybe_stop(stage, src_ap, waits):
            if stop_stage != stage:
                return
            for s_, v_ in waits:
                EW("g", s_, v_)
            npart = src_ap.ap[0][1]
            nfree = src_ap.ap[1][1]
            EI("g", lambda e: e.dma_start(
                out=stop_d[0:npart, 0:nfree], in_=src_ap), "OUT", 16)
            raise StopEmit()

        EI("y", lambda e: e.dma_start(out=identf[:, :], in_=ident_d[:, :]), "YC", 16)
        EW("y", "GG", 1)
        for row, src in ((0, ln1w_d), (6, ln1b_d), (12, ln2w_d), (18, ln2b_d), (24, bo_d)):
            EI("y", lambda e, r=row, s=src: e.dma_start(
                out=lnrows[r:r + 6, :], in_=s[:, :]), "YC", 16)
        EI("y", lambda e: e.dma_start(out=xa[0:8, 0:FF], in_=b1m_d[:, :]), "YC", 16)
        c_const_y = cnt["YC"]
        c_lnz = EI("g", lambda e: e.memset(lnrows[0:32, :], 0.0), "GG", 1)
        EI("g", lambda e: e.dma_start(
            out=wrb[:, :].rearrange("p (kt n) -> p kt n", n=NE),
            in_=wr_d.rearrange("(kt p) n -> p kt n", p=128)), "GC", 16)
        EI("g", lambda e: e.dma_start(out=brrow[:, :], in_=br_d[:, :]), "GC", 16)
        c_const_g = cnt["GC"]
        E("g", lambda e: e.memset(onesrow[:, :], 1.0))
        E("g", lambda e: e.memset(onescol[:, :], 1.0))
        E("g", lambda e: e.memset(negk0c[:, :], -K0))
        E("g", lambda e: e.memset(epscol[:, :], EPS))

        for b in range(2):
            for h in range(H):
                E("g", lambda e, b=b, h=h: e.memset(
                    vt[b][:, h * 129 + 128:h * 129 + 129], 1.0))
        c_memset = EI("g", lambda e: e.engine_nop(), "GG", 1)

        EW("t", "YC", c_const_y)
        EW("t", "GG", c_memset)
        for dt in range(NDT):
            pi = dt % 4
            psW(pi)
            mv = EI("t", lambda e, dt=dt, pi=pi: e.transpose(
                ps[pi][:, 0:32], lnrows[:, dt * 128:(dt + 1) * 128],
                identf[0:32, 0:32]), "MM", 1)
            psMW(pi, mv)
            psR(pi, "s")
            sv = EI("s", lambda e, dt=dt, pi=pi: e.copy(
                lncols[:, dt * 32:(dt + 1) * 32], ps[pi][:, 0:32]), "SS", 1)
            psMR(pi, "SS", sv)
        c_lnrows_mm = cnt["MM"]
        for ft in range(NFT):
            pi = ft % 4
            psW(pi)
            mv = EI("t", lambda e, ft=ft, pi=pi: e.transpose(
                ps[pi][:, 0:8], xa[0:8, ft * 128:(ft + 1) * 128],
                identf[0:8, 0:8]), "MM", 1)
            psMW(pi, mv)
            psR(pi, "v")
            vv = EI("v", lambda e, ft=ft, pi=pi: e.tensor_copy(
                b1c[:, ft * NE:(ft + 1) * NE], ps[pi][:, 0:8]), "VV", 1)
            psMR(pi, "VV", vv)
        c_b1c_vv = cnt["VV"]

        x_load = [0] * NT
        for i in range(2):
            x_load[i] = EI("y", lambda e, i=i: e.dma_start(
                out=x0b[i], in_=x_in[i * 128:(i + 1) * 128, :]), "F%d" % i, 16)
        xa_p0_ss = 0
        gb_rd_vv = 0
        tr_done = [0, 0]
        EW("s", "GG", c_memset)
        for i in range(NT):
            j = i % 2
            if i == 0:
                EW("s", "VV", c_b1c_vv)
            EW("v", "F%d" % j, x_load[i])
            VW(lambda e, j=j: e.reduce_sum(rcol[0][:, :], x0b[j], axis=AX.X))
            EW("s", "F%d" % j, x_load[i])
            EW("s", "VV", gb_rd_vv)
            sv1 = EI("s", lambda e, j=j: e.activation(
                gbuf, x0b[j], AF.Square), "SS", 1)
            EW("v", "SS", sv1)
            gb_rd_vv = VW(lambda e: e.reduce_sum(rcol[1][:, :], gbuf, axis=AX.X))
            VW(lambda e: e.tensor_scalar_mul(rcol[2][:, :], rcol[0][:, :], 1.0 / D))
            VW(lambda e: e.tensor_scalar_mul(rcol[3][:, :], rcol[1][:, :], 1.0 / D))
            VW(lambda e: e.tensor_mul(rcol[0][:, :], rcol[2][:, :], rcol[2][:, :]))
            vv2 = VW(lambda e: e.tensor_sub(rcol[1][:, :], rcol[3][:, :], rcol[0][:, :]))
            EW("s", "VV", vv2)
            EI("s", lambda e: e.activation(rcol[3][:, :], rcol[1][:, :], AF.Ln, bias=epscol[:, :]), "SS", 1)
            sv2 = EI("s", lambda e: e.activation(rcol[1][:, :], rcol[3][:, :], AF.Exp, scale=-0.5), "SS", 1)
            EW("v", "SS", sv2)
            VW(lambda e: e.tensor_mul(rcol[0][:, :], rcol[2][:, :], rcol[1][:, :]))
            VW(lambda e: e.tensor_scalar_mul(rcol[2][:, :], rcol[0][:, :], -1.0))
            if i >= 2:
                EW("v", "MM", tr_done[i % 2])
            vv3 = EI("v", lambda e, j=j: e.tensor_scalar(
                stb[j], x0b[j], rcol[1][:, :], rcol[2][:, :],
                ALU.mult, ALU.add), "VV", 1)
            if i + 2 < NT:
                EW("y", "VV", vv3)
                x_load[i + 2] = EI("y", lambda e, i=i: e.dma_start(
                    out=x0b[i % 2][:, :],
                    in_=x_in[(i + 2) * 128:(i + 3) * 128, :]), "F%d" % (i % 2), 16)
            for dt in range(NDT):
                pi = dt % 4
                psW(pi)
                EW("t", "VV", vv3)
                mv = EI("t", lambda e, dt=dt, pi=pi, j=j: e.transpose(
                    ps[pi][:, 0:128], stb[j][:, dt * 128:(dt + 1) * 128],
                    identf[:, :]), "MM", 1)
                psMW(pi, mv)
                psR(pi, "s")
                xa_p0_ss = EI("s", lambda e, i=i, dt=dt, pi=pi: e.copy(
                    xa[:, dt * T + i * 128:dt * T + (i + 1) * 128],
                    ps[pi][:, 0:128]), "SS", 1)
                psMR(pi, "SS", xa_p0_ss)
                EW("v", "SS", xa_p0_ss)
                vv4 = EI("v", lambda e, i=i, dt=dt, pi=pi: e.tensor_copy(
                    xnT[:, dt * T + i * 128:dt * T + (i + 1) * 128],
                    ps[pi][:, 0:128]), "VV", 1)
                psMR(pi, "VV", vv4)
            tr_done[i % 2] = cnt["MM"]
        c_ln0_ss = cnt["SS"]
        c_ln0_vv = cnt["VV"]

        EW("y", "VV", c_ln0_vv)
        c_xn0_spill = EI("y", lambda e: e.dma_start(
            out=xn0_dram[:, :], in_=xnT[:, :]), "SP", 16)

        EW("t", "GC", c_const_g)
        EW("t", "GG", c_memset)
        lgt_ss = 0
        for i in range(NT):
            pi = i % 2
            psW(pi)
            for dt in range(NDT):
                E("t", lambda e, i=i, dt=dt, pi=pi: e.matmul(
                    ps[pi][:, 0:NE],
                    xnT[:, dt * T + i * 128:dt * T + (i + 1) * 128],
                    wrb[:, dt * NE:(dt + 1) * NE],
                    start=(dt == 0), stop=False))
            mv = EI("t", lambda e, pi=pi: e.matmul(
                ps[pi][:, 0:NE], onesrow[0:1, 0:128], brrow[0:1, :],
                start=False, stop=True), "MM", 1)
            psMW(pi, mv)
            psR(pi, "s")
            lgt_ss = EI("s", lambda e, i=i, pi=pi: e.copy(
                lgt[:, i * NE:(i + 1) * NE], ps[pi][:, 0:NE]), "SS", 1)
            psMR(pi, "SS", lgt_ss)
        BIG = 1e30
        EW("v", "SS", lgt_ss)
        rr = [gbuf[:, 128 + k * 16:128 + (k + 1) * 16] for k in range(4)]

        def step(fn, dep=True):
            for i in range(NT):
                L = lgt[:, i * NE:(i + 1) * NE]
                W = wrt[:, i * NE:(i + 1) * NE]
                EI("v", lambda e, i=i, L=L, W=W: fn(e, i, L, W), "VV", 1)
            v = cnt["VV"]
            EW("v", "VV", v)
            return v

        step(lambda e, i, L, W: e.reduce_max(rr[0][:, i:i + 1], L, axis=AX.X, negate=True))
        step(lambda e, i, L, W: e.tensor_scalar(
            W, L, rr[0][:, i:i + 1], 0.0, ALU.add, ALU.is_equal))
        step(lambda e, i, L, W: e.tensor_scalar_mul(W, W, -BIG))
        step(lambda e, i, L, W: e.tensor_add(W, W, L))
        step(lambda e, i, L, W: e.reduce_max(rr[1][:, i:i + 1], W, axis=AX.X, negate=True))
        vsel = step(lambda e, i, L, W: e.tensor_scalar(
            W, L, rr[1][:, i:i + 1], 0.0, ALU.add, ALU.is_ge))
        EW("s", "VV", vsel)
        sv = 0
        for i in range(NT):
            L = lgt[:, i * NE:(i + 1) * NE]
            sv = EI("s", lambda e, i=i, L=L: e.activation(
                gbuf[:, i * NE:(i + 1) * NE], L, AF.Exp,
                bias=rr[0][:, i:i + 1]), "SS", 1)
        EW("v", "SS", sv)
        step(lambda e, i, L, W: e.tensor_mul(W, W, gbuf[:, i * NE:(i + 1) * NE]))
        step(lambda e, i, L, W: e.reduce_sum(rr[2][:, i:i + 1], W, axis=AX.X))
        step(lambda e, i, L, W: e.reciprocal(rr[3][:, i:i + 1], rr[2][:, i:i + 1]))
        step(lambda e, i, L, W: e.tensor_scalar_mul(W, W, rr[3][:, i:i + 1]))
        c_router_vv = cnt["VV"]

        wq_rdy = load_mat_1024([hs4[:, 0:4096], hs4[:, 4096:8192]],
                               ("W4", "W5"), wq_d, 0, wait_mm=c_lnrows_mm)
        wk_rdy = load_mat_1024(hs[2:4], ("W2", "W3"), wk_d, 0)
        wv_rdy0 = load_mat_1024(hs[0:2], ("W0", "W1"), wv_d, 0)

        def emit_ln(l, which, waits):
            wofs, bofs = (0, 6) if which == 0 else (12, 18)
            rset = [
                dict(mf=mrow_f[:, :], mb=mrow_b[:, :], rr=rrow_b[:, :],
                     v1=vrow[:, :], m2=arow[:, :]),
                dict(mf=mf_B, mb=mb_B, rr=rr_B, v1=v1_B, m2=m2_B),
            ]
            sums_bank = [(6, 7), (2, 3)]
            bc_bank = [(4, 5), (0, 1)]
            apply_vv = [0] * NCH
            mm_sums = [0] * NCH
            rows_ss = [0] * NCH
            first = [True]

            def emit_sums(c):
                b0, b1 = sums_bank[c % 2]
                psW(b0)
                psW(b1)
                sq_mm = {}
                for dt in range(NDT):
                    sl = slice(dt * T + c * 512, dt * T + c * 512 + 512)
                    if first[0]:
                        for s_, v_ in waits:
                            EW("s", s_, v_)
                        first[0] = False
                    vv = EI("s", lambda e, sl=sl: e.copy(
                        xnT[:, sl], xa[:, sl]), "SS", 1)
                    if dt >= 2:
                        EW("s", "MM", sq_mm[dt - 2])
                    sv = EI("s", lambda e, sl=sl, dt=dt: e.activation(
                        xsq[dt % 2][:, :], xnT[:, sl], AF.Square), "SS", 1)
                    EW("t", "SS", vv)
                    E("t", lambda e, sl=sl, dt=dt, b0=b0: e.matmul(
                        ps[b0][0:1, :], onescol[:, :], xnT[:, sl],
                        start=(dt == 0), stop=(dt == NDT - 1),
                        skip_group_check=True))
                    EW("t", "SS", sv)
                    sq_mm[dt] = EI("t", lambda e, dt=dt, b1=b1: e.matmul(
                        ps[b1][0:1, :], onescol[:, :], xsq[dt % 2][:, :],
                        start=(dt == 0), stop=(dt == NDT - 1),
                        skip_group_check=True), "MM", 1)
                mm_sums[c] = cnt["MM"]
                psMW(b0, mm_sums[c])
                psMW(b1, mm_sums[c])

            def emit_rows(c):
                b0, b1 = sums_bank[c % 2]
                r = rset[c % 2]
                psR(b0, "s")
                psR(b1, "s")
                EI("s", lambda e, r=r, b0=b0: e.activation(
                    r["mf"], ps[b0][0:1, :], AF.Copy, scale=1.0 / D), "SS", 1)
                EI("s", lambda e, r=r, b0=b0: e.activation(
                    r["mb"], ps[b0][0:1, :], AF.Copy, scale=1.0 / D), "SS", 1)
                EI("s", lambda e, r=r, b1=b1: e.activation(
                    r["v1"], ps[b1][0:1, :], AF.Copy, scale=1.0 / D), "SS", 1)
                sv2 = EI("s", lambda e, r=r: e.activation(
                    r["m2"], r["mf"], AF.Square), "SS", 1)
                psMR(b0, "SS", sv2)
                psMR(b1, "SS", sv2)
                EW("g", "SS", sv2)
                vv2 = EI("g", lambda e, r=r: e.tensor_sub(r["v1"], r["v1"], r["m2"]), "GG", 1)
                EW("s", "GG", vv2)
                EI("s", lambda e, r=r: e.activation(
                    r["v1"], r["v1"], AF.Ln, bias=epscol[0:1, :]), "SS", 1)
                rows_ss[c] = EI("s", lambda e, r=r: e.activation(
                    r["rr"], r["v1"], AF.Exp, scale=-0.5), "SS", 1)

            def emit_bcast_apply(c):
                r = rset[c % 2]
                p0, p1 = bc_bank[c % 2]
                psW(p0)
                psW(p1)
                EW("t", "SS", rows_ss[c])
                E("t", lambda e, r=r, p0=p0: e.matmul(
                    ps[p0][:, :], onesrow[0:1, 0:128], r["mb"],
                    start=True, stop=True))
                mmb = EI("t", lambda e, r=r, p1=p1: e.matmul(
                    ps[p1][:, :], onesrow[0:1, 0:128], r["rr"],
                    start=True, stop=True), "MM", 1)
                psMW(p0, mmb)
                psMW(p1, mmb)
                psR(p0, "v")
                psR(p1, "v")
                vvm = 0
                ssm = 0
                for dt in range(NDT):
                    sl = slice(dt * T + c * 512, dt * T + c * 512 + 512)
                    EI("v", lambda e, sl=sl, p0=p0: e.tensor_sub(
                        aps1[:, :], xa[:, sl], ps[p0][:, :]), "VV", 1)
                    if dt >= 2 and dt % 2 == 0:
                        EW("v", "SS", ssm)
                    vvm = EI("v", lambda e, p1=p1: e.tensor_mul(
                        aps2[:, :], aps1[:, :], ps[p1][:, :]), "VV", 1)
                    wc = lncols[:, dt * 32 + wofs + l:dt * 32 + wofs + l + 1]
                    bc = lncols[:, dt * 32 + bofs + l:dt * 32 + bofs + l + 1]
                    if dt % 2 == 0:
                        EI("v", lambda e, sl=sl, wc=wc, bc=bc: e.tensor_scalar(
                            xnT[:, sl], aps2[:, :], wc, bc, ALU.mult, ALU.add), "VV", 1)
                    else:
                        EW("s", "VV", vvm)
                        ssm = EI("s", lambda e, sl=sl, wc=wc, bc=bc: e.activation(
                            xnT[:, sl], aps2[:, :], AF.Identity,
                            scale=wc, bias=bc), "SS", 1)
                psMR(p0, "VV", vvm)
                psMR(p1, "VV", vvm)
                apply_vv[c] = ssm

            for c in range(NCH):
                emit_sums(c)
                if c >= 1:
                    emit_rows(c - 1)
                    emit_bcast_apply(c - 1)
            emit_rows(NCH - 1)
            emit_bcast_apply(NCH - 1)
            return apply_vv

        def emit_ffn(l, ln_avv, free_mm_hs01, free_mm_hs23):
            NBLK = NFT // 4
            w1_rdy = [0] * NBLK
            w2_rdy = [0] * NBLK
            h_mm = [0] * NBLK
            y_mm = [0] * (NBLK // 2)

            def w1sl(B, fl):
                return hs[B % 2][:, fl * 1024:(fl + 1) * 1024]

            def w2sl(B, fl):
                return hs[2 + B % 2][:, fl * 1024:(fl + 1) * 1024]

            def load_blk_w1(B, wmm):
                s1 = "W%d" % (B % 2)
                if wmm:
                    EW("g", "MM", wmm)
                for fl in range(4):
                    ft = B * 4 + fl
                    src = w1_d[l, :, ft * 128:(ft + 1) * 128].rearrange(
                        "(kt p) m -> p kt m", p=128)
                    dst = w1sl(B, fl).rearrange("p (kt m) -> p kt m", m=128)
                    EI("g", lambda e, d=dst, s=src:
                       e.dma_start(out=d, in_=s), s1, 16)
                w1_rdy[B] = (s1, cnt[s1])

            def load_blk_w2(B, wmm2):
                s2 = "W%d" % (2 + B % 2)
                if wmm2:
                    EW("g", "MM", wmm2)
                for fl in range(4):
                    ft = B * 4 + fl
                    src = w2_d[l, ft * 128:(ft + 1) * 128, :]
                    EI("g", lambda e, d=w2sl(B, fl), s=src:
                       e.dma_start(out=d, in_=s), s2, 16)
                w2_rdy[B] = (s2, cnt[s2])

            load_blk_w1(0, free_mm_hs01)
            load_blk_w2(0, free_mm_hs23)
            load_blk_w1(1, free_mm_hs01)
            load_blk_w2(1, free_mm_hs23)
            y_vv = 0
            relu_ss = {}
            lrelu_vv = [0]
            relu_sem = "SS" if use_lrelu else "VV"
            for S in range(NBLK // 2):
                for B in (2 * S, 2 * S + 1):
                    EW("t", *w1_rdy[B])
                    half = (B % 2) * (H * T // 2)
                    for fl in range(4):
                        for c in range(NCH):
                            if B == 0 and fl == 0:
                                EW("t", "SS", ln_avv[c] if use_lrelu else ln_avv[-1])
                            pi = 4 + (fl * NCH + c) % 2
                            psW(pi)
                            for dt in range(NDT - 1):
                                E("t", lambda e, B=B, fl=fl, c=c, dt=dt, pi=pi: e.matmul(
                                    ps[pi][:, :], w1sl(B, fl)[:, dt * 128:(dt + 1) * 128],
                                    xnT[:, dt * T + c * 512:dt * T + c * 512 + 512],
                                    start=(dt == 0), stop=False))
                            mv = EI("t", lambda e, B=B, fl=fl, c=c, pi=pi: e.matmul(
                                ps[pi][:, :], w1sl(B, fl)[:, 7 * 128:8 * 128],
                                xnT[:, 7 * T + c * 512:7 * T + c * 512 + 512],
                                start=False, stop=True), "MM", 1)
                            psMW(pi, mv)
                            hsl = slice(half + fl * T + c * 512,
                                        half + fl * T + c * 512 + 512)
                            if use_lrelu:
                                psR(pi, "s")
                                sv = EI("s", lambda e, hsl=hsl, pi=pi:
                                        e.activation(eqT[:, hsl], ps[pi][:, :],
                                                     AF.Lrelu, alpha=0.01), "SS", 1)
                                psMR(pi, "SS", sv)
                            else:
                                psR(pi, "s")
                                EW("s", "VV", lrelu_vv[0])
                                s1 = EI("s", lambda e, pi=pi: e.activation(
                                    aps1[:, :], ps[pi][:, :], AF.Copy,
                                    scale=0.01), "SS", 1)
                                psMR(pi, "SS", s1)
                                psR(pi, "v")
                                EW("v", "SS", s1)
                                sv = EI("v", lambda e, hsl=hsl, pi=pi:
                                        e.tensor_max(eqT[:, hsl], ps[pi][:, :],
                                                     aps1[:, :]), "VV", 1)
                                psMR(pi, "VV", sv)
                                lrelu_vv[0] = sv
                            relu_ss[(B, fl, c)] = sv
                    h_mm[B] = cnt["MM"]
                    if B == 1:
                        maybe_stop("h01", eqT[:, :], [(relu_sem,
                                    relu_ss[(1, 3, NCH - 1)])])
                    if B + 2 < NBLK:
                        load_blk_w1(B + 2, h_mm[B])
                EW("t", *w2_rdy[2 * S])
                EW("t", *w2_rdy[2 * S + 1])
                EW("t", relu_sem, relu_ss[(2 * S + 1, 3, NCH - 1)])
                for ot in range(NDT):
                    for c in range(NCH):
                        pi = (ot * NCH + c) % 4
                        psW(pi)
                        n = 0
                        for B in (2 * S, 2 * S + 1):
                            half = (B % 2) * (H * T // 2)
                            for fl in range(4):
                                emitfn = (lambda e, B=B, fl=fl, ot=ot, c=c, pi=pi,
                                          half=half, n=n: e.matmul(
                                    ps[pi][:, :],
                                    w2sl(B, fl)[:, ot * 128:(ot + 1) * 128],
                                    eqT[:, half + fl * T + c * 512:
                                        half + fl * T + c * 512 + 512],
                                    start=(n == 0), stop=(n == 7)))
                                if n == 7:
                                    mv = EI("t", emitfn, "MM", 1)
                                else:
                                    E("t", emitfn)
                                n += 1
                        psMW(pi, mv)
                        psR(pi, "v")
                        y_vv = EI("v", lambda e, ot=ot, c=c, pi=pi: e.tensor_add(
                            xa[:, ot * T + c * 512:ot * T + c * 512 + 512],
                            xa[:, ot * T + c * 512:ot * T + c * 512 + 512],
                            ps[pi][:, :]), "VV", 1)
                        psMR(pi, "VV", y_vv)
                y_mm[S] = cnt["MM"]
                for B2 in (2 * S + 2, 2 * S + 3):
                    if B2 < NBLK:
                        load_blk_w2(B2, y_mm[S])
                if S == 0:
                    maybe_stop("y0", xa[:, :], [("VV", y_vv)])
            return y_vv, h_mm[NBLK - 1], y_mm[-1]

        xnT_free = 0
        spkg_rd_v = 0
        xa_ready = [("SS", c_ln0_ss)]
        eqT_free_mm = 0
        wq_ready, wk_ready, wv_ready = wq_rdy, wk_rdy, wv_rdy0
        dbg_done = []

        try:
            maybe_stop("p0x", xa[:, :], [("SS", c_ln0_ss)])
            for l in range(depth):
                ln1_avv = emit_ln(l, 0, xa_ready + [("MM", xnT_free)])
                ln1_vv = ln1_avv[-1]
                maybe_stop("ln1", xnT[:, :], [("SS", ln1_vv)])

                EWL("t", wk_ready)
                psW(5)
                psW(6)
                psW(7)
                ek_ss = [0, 0]
                vt_vv = [0, 0]
                ekv_rd = [0, 0]
                for i in range(NT):
                    j = i % 2
                    if i % 4 == 0:
                        EW("t", "SS", ln1_avv[i // 4])
                    for nch in range(2):
                        pi = nch
                        psW(pi)
                        for dt in range(NDT):
                            wsl = hs[2 + dt // 4][:, (dt % 4) * 1024 + nch * 512:
                                                  (dt % 4) * 1024 + nch * 512 + 512]
                            emitfn = (lambda e, i=i, dt=dt, pi=pi, wsl=wsl: e.matmul(
                                ps[pi][:, :],
                                xnT[:, dt * T + i * 128:dt * T + (i + 1) * 128],
                                wsl, start=(dt == 0), stop=(dt == NDT - 1)))
                            if dt == NDT - 1:
                                mv = EI("t", emitfn, "MM", 1)
                            else:
                                E("t", emitfn)
                        psMW(pi, mv)
                    if i == 0:
                        EWL("t", wv_ready)
                    for nch in range(2):
                        pi = 2 + nch
                        psW(pi)
                        for dt in range(NDT):
                            wsl = hs[dt // 4][:, (dt % 4) * 1024 + nch * 512:
                                              (dt % 4) * 1024 + nch * 512 + 512]
                            emitfn = (lambda e, i=i, dt=dt, pi=pi, wsl=wsl: e.matmul(
                                ps[pi][:, :],
                                xnT[:, dt * T + i * 128:dt * T + (i + 1) * 128],
                                wsl, start=(dt == 0), stop=(dt == NDT - 1)))
                            if dt == NDT - 1:
                                mv = EI("t", emitfn, "MM", 1)
                            else:
                                E("t", emitfn)
                        psMW(pi, mv)
                    if i >= 2:
                        EW("s", "MM", ekv_rd[j])
                    psR(0, "s")
                    EI("s", lambda e, j=j: e.activation(
                        ek[j][:, 0:512], ps[0][:, :], AF.Exp, bias=negk0c[:, :]), "SS", 1)
                    psR(1, "s")
                    ek_ss[j] = EI("s", lambda e, j=j: e.activation(
                        ek[j][:, 512:1024], ps[1][:, :], AF.Exp, bias=negk0c[:, :]), "SS", 1)
                    psMR(0, "SS", ek_ss[j])
                    psMR(1, "SS", ek_ss[j])
                    if i >= 2:
                        EW("v", "MM", ekv_rd[j])
                    psR(2, "v")
                    EI("v", lambda e, j=j: e.tensor_copy(
                        vt[j][:, 0:4 * 129].rearrange("p (h e) -> p h e", e=129)[:, :, 0:128],
                        ps[2][:, :].rearrange("p (h e) -> p h e", e=128)), "VV", 1)
                    psR(3, "v")
                    vt_vv[j] = EI("v", lambda e, j=j: e.tensor_copy(
                        vt[j][:, 4 * 129:8 * 129].rearrange("p (h e) -> p h e", e=129)[:, :, 0:128],
                        ps[3][:, :].rearrange("p (h e) -> p h e", e=128)), "VV", 1)
                    psMR(2, "VV", vt_vv[j])
                    psMR(3, "VV", vt_vv[j])
                    EW("t", "SS", ek_ss[j])
                    EW("t", "VV", vt_vv[j])
                    for h in range(H):
                        pi = 5 + h // 3
                        co = (h % 3) * 129
                        emitfn = (lambda e, j=j, h=h, pi=pi, co=co, i=i: e.matmul(
                            ps[pi][:, co:co + 129],
                            ek[j][:, h * DH:(h + 1) * DH],
                            vt[j][:, h * 129:(h + 1) * 129],
                            start=(i == 0 and h % 3 == 0), stop=(i == NT - 1),
                            skip_group_check=True))
                        if h == H - 1:
                            ekv_rd[j] = EI("t", emitfn, "MM", 1)
                        else:
                            E("t", emitfn)
                s_acc_mm = cnt["MM"]
                for b in (5, 6, 7):
                    psMW(b, s_acc_mm)

                EW("s", "VV", spkg_rd_v)
                pk_ss = 0
                for h in range(H):
                    pi = 5 + h // 3
                    co = (h % 3) * 129
                    psR(pi, "s")
                    pk_ss = EI("s", lambda e, h=h, pi=pi, co=co: e.copy(
                        spkg[:, h * 129:(h + 1) * 129], ps[pi][:, co:co + 129]), "SS", 1)
                    psMR(pi, "SS", pk_ss)
                maybe_stop("spkgl", spkg[:, :], [("SS", pk_ss)])
                EW("g", "SS", pk_ss)
                dwi = EI("g", lambda e: e.dma_start(out=cc_in[:, :], in_=spkg[:, :]), "CD", 16)
                EW("g", "CD", dwi)
                ccv = EI("g", lambda e: e.collective_compute(
                    "AllReduce", ALU.add,
                    replica_groups=[[0, 1], [2, 3], [4, 5], [6, 7]],
                    ins=[cc_in.ap().opt()], outs=[cc_out.ap().opt()]), "CC", 1)
                wo_ready = load_mat_1024(hs[2:4], ("W2", "W3"), wo_d, l, wait_mm=s_acc_mm)
                EW("g", "CC", ccv)
                dwo = EI("g", lambda e: e.dma_start(out=spkg[:, :], in_=cc_out[:, :]), "CD", 16)

                EWL("t", wq_ready)
                eq_ss = 0
                for h in range(H):
                    for c in range(NCH):
                        pi = (h * NCH + c) % 4
                        psW(pi)
                        for dt in range(NDT):
                            wsl = hs4[:, (dt // 4) * 4096 + (dt % 4) * 1024 + h * 128:
                                      (dt // 4) * 4096 + (dt % 4) * 1024 + (h + 1) * 128]
                            emitfn = (lambda e, dt=dt, c=c, pi=pi, wsl=wsl, h=h: e.matmul(
                                ps[pi][:, :], wsl,
                                xnT[:, dt * T + c * 512:dt * T + c * 512 + 512],
                                start=(dt == 0), stop=(dt == NDT - 1)))
                            if dt == NDT - 1:
                                mv = EI("t", emitfn, "MM", 1)
                            else:
                                E("t", emitfn)
                        psMW(pi, mv)
                        psR(pi, "s")
                        eq_ss = EI("s", lambda e, h=h, c=c, pi=pi: e.activation(
                            eqT[:, h * T + c * 512:h * T + c * 512 + 512],
                            ps[pi][:, :], AF.Exp, bias=negk0c[:, :]), "SS", 1)
                        psMR(pi, "SS", eq_ss)
                q_done_mm = cnt["MM"]
                maybe_stop("eq", eqT[:, :], [("SS", eq_ss)])

                EW("v", "CD", dwo)
                ctx_vv = 0
                for h in range(H):
                    VW(lambda e, h=h: e.reciprocal(
                        rcol[0][:, :], spkg[:, h * 129 + 128:h * 129 + 129]))
                    ctx_vv = EI("v", lambda e, h=h: e.tensor_scalar_mul(
                        ctxbf[:, h * DH:(h + 1) * DH],
                        spkg[:, h * 129:h * 129 + 128], rcol[0][:, :]), "VV", 1)
                spkg_rd_v = ctx_vv
                maybe_stop("spkg", spkg[:, :], [("CD", dwo)])
                maybe_stop("ctx", ctxbf[:, :], [("VV", ctx_vv)])

                EW("t", "VV", ctx_vv)
                EW("t", "SS", eq_ss)
                attn_vv = 0
                attn_c_vv = [0] * NCH
                xa_wo_vv = 0

                wo_q = []
                wo_rot = [2, 3, 5]

                def emit_wo_tile(cw, ot, ridx):
                    nonlocal xa_wo_vv
                    pi = wo_rot[ridx % 3]
                    psW(pi)
                    if ot == 0:
                        EW("t", "VV", attn_c_vv[cw])
                    for h in range(H):
                        wsl = hs[2 + h // 4][:, (h % 4) * 1024 + ot * 128:
                                             (h % 4) * 1024 + (ot + 1) * 128]
                        emitfn = (lambda e, h=h, cw=cw, pi=pi, wsl=wsl: e.matmul(
                            ps[pi][:, :], wsl,
                            eqT[:, h * T + cw * 512:h * T + cw * 512 + 512],
                            start=(h == 0), stop=(h == H - 1)))
                        if h == H - 1:
                            mv2 = EI("t", emitfn, "MM", 1)
                        else:
                            E("t", emitfn)
                    psMW(pi, mv2)
                    psR(pi, "v")
                    boc = lncols[:, ot * 32 + 24 + l:ot * 32 + 24 + l + 1]
                    xa_wo_vv = EI("v", lambda e, ot=ot, cw=cw, pi=pi, boc=boc:
                                  e.scalar_tensor_tensor(
                                      xa[:, ot * T + cw * 512:ot * T + cw * 512 + 512],
                                      ps[pi][:, :], boc,
                                      xa[:, ot * T + cw * 512:ot * T + cw * 512 + 512],
                                      ALU.add, ALU.add), "VV", 1)
                    psMR(pi, "VV", xa_wo_vv)

                wo_ridx = 0
                for c in range(NCH):
                    for h in range(H):
                        pb = 6 + h % 2
                        psW(4)
                        mvz = EI("t", lambda e, h=h, c=c: e.matmul(
                            ps[4][0:1, :], onescol[:, :],
                            eqT[:, h * T + c * 512:h * T + c * 512 + 512],
                            start=True, stop=True), "MM", 1)
                        psMW(4, mvz)
                        psR(4, "s")
                        EI("s", lambda e: e.activation(arow[:, :], ps[4][0:1, :], AF.Ln), "SS", 1)
                        svr = EI("s", lambda e: e.activation(
                            mrow_b[:, :], arow[:, :], AF.Exp, scale=-1.0), "SS", 1)
                        psMR(4, "SS", svr)
                        if wo_q:
                            cw, ot = wo_q.pop(0)
                            emit_wo_tile(cw, ot, wo_ridx)
                            wo_ridx += 1
                        psW(pb)
                        EW("t", "SS", svr)
                        mvb = EI("t", lambda e, pb=pb: e.matmul(
                            ps[pb][:, :], onesrow[0:1, 0:128], mrow_b[:, :],
                            start=True, stop=True), "MM", 1)
                        psMW(pb, mvb)
                        pi = h % 2
                        psW(pi)
                        mvc = EI("t", lambda e, h=h, c=c, pi=pi: e.matmul(
                            ps[pi][:, :], ctxbf[:, h * DH:(h + 1) * DH],
                            eqT[:, h * T + c * 512:h * T + c * 512 + 512],
                            start=True, stop=True), "MM", 1)
                        psMW(pi, mvc)
                        psR(pi, "s")
                        EW("s", "VV", attn_vv)
                        sva = EI("s", lambda e, pi=pi: e.copy(aps2[:, :], ps[pi][:, :]), "SS", 1)
                        psMR(pi, "SS", sva)
                        EW("v", "SS", sva)
                        psR(pb, "v")
                        attn_vv = EI("v", lambda e, h=h, c=c, pb=pb: e.tensor_mul(
                            eqT[:, h * T + c * 512:h * T + c * 512 + 512],
                            aps2[:, :], ps[pb][:, :]), "VV", 1)
                        psMR(pb, "VV", attn_vv)
                    attn_c_vv[c] = attn_vv
                    wo_q.extend((c, ot) for ot in range(NDT))
                maybe_stop("attn", eqT[:, :], [("VV", attn_vv)])
                for cw, ot in wo_q:
                    emit_wo_tile(cw, ot, wo_ridx)
                    wo_ridx += 1
                wo_done_mm = cnt["MM"]
                xnT_free = s_acc_mm
                maybe_stop("wo", xa[:, :], [("VV", xa_wo_vv)])

                ln2_avv = emit_ln(l, 1, [("VV", xa_wo_vv), ("MM", q_done_mm)])
                maybe_stop("ln2", xnT[:, :], [("SS", ln2_avv[-1])])

                ffn_y_vv, ffn_h_mm, ffn_y_mm = emit_ffn(
                    l, ln2_avv, free_mm_hs01=s_acc_mm, free_mm_hs23=wo_done_mm)

                xa_ready = [("VV", ffn_y_vv)]
                xnT_free = ffn_h_mm
                if l + 1 < depth:
                    wq_ready = load_mat_1024(
                        [hs4[:, 0:4096], hs4[:, 4096:8192]], ("W4", "W5"),
                        wq_d, l + 1, wait_mm=q_done_mm)
                    wv_ready = load_mat_1024(hs[0:2], ("W0", "W1"), wv_d, l + 1,
                                             wait_mm=ffn_h_mm)
                    wk_ready = load_mat_1024(hs[2:4], ("W2", "W3"), wk_d, l + 1,
                                             wait_mm=ffn_y_mm)
                if ("L%d" % l) in dbg_t:
                    EW("y", "VV", ffn_y_vv)
                    dv = EI("y", lambda e, l=l: e.dma_start(
                        out=dbg_t["L%d" % l][:, :], in_=xa[:, :]), "OUT", 16)
                    dbg_done.append(dv)

            if "p0" in dbg_t:
                EW("y", "SS", c_ln0_ss)
                dbg_done.append(EI("y", lambda e: e.dma_start(
                    out=dbg_t["p0"][:, :], in_=xa[:, :]), "OUT", 16))
            if "wrt" in dbg_t:
                EW("y", "VV", c_router_vv)
                dbg_done.append(EI("y", lambda e: e.dma_start(
                    out=dbg_t["wrt"][:, :], in_=wrt[:, :]), "OUT", 16))

            if do_moe:
                EW("y", "MM", xnT_free)
                xn0_rdy = EI("y", lambda e: e.dma_start(
                    out=xnT[:, :], in_=xn0_dram[:, :]), "F0", 16)

                spl_dx = 0
                for i in range(NT):
                    j = i % 2
                    for dt in range(NDT):
                        pi = dt % 4
                        psW(pi)
                        if i == 0 and dt == 0:
                            for s_, v_ in xa_ready:
                                EW("t", s_, v_)
                        mv = EI("t", lambda e, i=i, dt=dt, pi=pi: e.transpose(
                            ps[pi][:, 0:128],
                            xa[:, dt * T + i * 128:dt * T + (i + 1) * 128],
                            identf[:, :]), "MM", 1)
                        psMW(pi, mv)
                        psR(pi, "s")
                        if dt == 0 and i >= 2:
                            EW("s", "SP", spl_dx)
                        sv = EI("s", lambda e, j=j, dt=dt, pi=pi: e.copy(
                            stb[j][:, dt * 128:(dt + 1) * 128], ps[pi][:, 0:128]), "SS", 1)
                        psMR(pi, "SS", sv)
                    EW("y", "SS", cnt["SS"])
                    spl_dx = EI("y", lambda e, i=i, j=j: e.dma_start(
                        out=s_dram[i * 128:(i + 1) * 128, :], in_=stb[j]), "SP", 16)
                s_spill_mm = cnt["MM"]
                s_spill_dx = spl_dx


                NBLK = NFT // 4
                h_mm = [0] * NBLK
                y_mm_s = [0] * (NBLK // 2)
                w1_rdy = [0] * NBLK
                w2_rdy = [0] * NBLK
                b2_ss_mm = 0
                first_y_guard = [("MM", s_spill_mm), ("SP", s_spill_dx)]

                for ex in range(NE):
                    EW("g", "MM", b2_ss_mm)
                    b2_rdy = EI("g", lambda e, ex=ex: e.dma_start(
                        out=b2row[:, :], in_=b2m_d[ex:ex + 1, :]), "B2", 16)

                    def load_w1_m(B, wmm, ex=ex):
                        if wmm:
                            EW("g", "MM", wmm)
                        srcw = W1m_d[ex, :, B * 512:(B + 1) * 512].rearrange(
                            "(kt p) m -> p kt m", p=128)
                        dstw = hs[B % 2][:, :].rearrange("p (kt m) -> p kt m", m=512)
                        EI("g", lambda e, d=dstw,
                           s=srcw: e.dma_start(out=d, in_=s), "W%d" % (B % 2), 16)
                        w1_rdy[B] = ("W%d" % (B % 2), cnt["W%d" % (B % 2)])

                    def load_w2_m(B, wmm2, ex=ex):
                        if wmm2:
                            EW("g", "MM", wmm2)
                        for fl in range(4):
                            ft = B * 4 + fl
                            src = W2m_d[ex, ft * 128:(ft + 1) * 128, :]
                            EI("g", lambda e, d=hs[2 + B % 2][:, fl * 1024:(fl + 1) * 1024],
                               s=src: e.dma_start(out=d, in_=s), "W%d" % (2 + B % 2), 16)
                        w2_rdy[B] = ("W%d" % (2 + B % 2), cnt["W%d" % (2 + B % 2)])

                    init_w1a = (ffn_h_mm if (depth > 0 and ex == 0) else
                                (h_mm[NBLK - 2] if ex > 0 else 0))
                    init_w1b = (ffn_h_mm if (depth > 0 and ex == 0) else
                                (h_mm[NBLK - 1] if ex > 0 else 0))
                    init_w2 = (ffn_y_mm if (depth > 0 and ex == 0) else
                               (y_mm_s[-1] if ex > 0 else 0))
                    load_w1_m(0, init_w1a)
                    load_w2_m(0, init_w2)
                    load_w1_m(1, init_w1b)
                    load_w2_m(1, init_w2)
                    relu_ss = {}
                    for S in range(NBLK // 2):
                        for B in (2 * S, 2 * S + 1):
                            EW("t", *w1_rdy[B])
                            if ex == 0 and B == 0:
                                EW("t", "F0", xn0_rdy)
                            half = (B % 2) * (H * T // 2)
                            for fl in range(4):
                                ft = B * 4 + fl
                                for c in range(NCH):
                                    pi = 4 + (fl * NCH + c) % 2
                                    psW(pi)
                                    for dt in range(NDT):
                                        emitfn = (lambda e, fl=fl, c=c, dt=dt, pi=pi, B=B:
                                          e.matmul(
                                            ps[pi][:, :],
                                            hs[B % 2][:, dt * 512 + fl * 128:
                                                      dt * 512 + (fl + 1) * 128],
                                            xnT[:, dt * T + c * 512:dt * T + c * 512 + 512],
                                            start=(dt == 0), stop=(dt == NDT - 1)))
                                        if dt == NDT - 1:
                                            mv = EI("t", emitfn, "MM", 1)
                                        else:
                                            E("t", emitfn)
                                    psMW(pi, mv)
                                    psR(pi, "s")
                                    if ex == 0 and B == 0 and fl == 0 and c == 0:
                                        EW("s", "SP", s_spill_dx)
                                    bcol = b1c[:, ft * NE + ex:ft * NE + ex + 1]
                                    sv = EI("s", lambda e, half=half, fl=fl, c=c, pi=pi,
                                            bcol=bcol: e.activation(
                                                eqT[:, half + fl * T + c * 512:
                                                    half + fl * T + c * 512 + 512],
                                                ps[pi][:, :], AF.Relu, bias=bcol), "SS", 1)
                                    psMR(pi, "SS", sv)
                                    relu_ss[(B, fl, c)] = sv
                            h_mm[B] = cnt["MM"]
                            if B + 2 < NBLK:
                                load_w1_m(B + 2, h_mm[B])
                        EW("t", *w2_rdy[2 * S])
                        EW("t", *w2_rdy[2 * S + 1])
                        if S == 0:
                            EW("t", "B2", b2_rdy)
                        EW("t", "SS", relu_ss[(2 * S + 1, 3, NCH - 1)])
                        for tt in range(NT):
                            for oc in range(2):
                                pi = (tt * 2 + oc) % 4
                                psW(pi)
                                n = 0
                                for B in (2 * S, 2 * S + 1):
                                    half = (B % 2) * (H * T // 2)
                                    for fl in range(4):
                                        emitfn = (lambda e, B=B, fl=fl, tt=tt, oc=oc, pi=pi,
                                            half=half, n=n, S=S: e.matmul(
                                            ps[pi][:, :],
                                            eqT[:, half + fl * T + tt * 128:
                                                half + fl * T + (tt + 1) * 128],
                                            hs[2 + B % 2][:, fl * 1024 + oc * 512:
                                                          fl * 1024 + oc * 512 + 512],
                                            start=(n == 0), stop=(n == 7 and S != 0)))
                                        if n == 7 and S != 0:
                                            mv = EI("t", emitfn, "MM", 1)
                                        else:
                                            E("t", emitfn)
                                        n += 1
                                if S == 0:
                                    mv = EI("t", lambda e, pi=pi, oc=oc: e.matmul(
                                        ps[pi][:, :], onesrow[0:1, 0:128],
                                        b2row[0:1, oc * 512:oc * 512 + 512],
                                        start=False, stop=True), "MM", 1)
                                psMW(pi, mv)
                                psR(pi, "v")
                                if ex == 0 and S == 0 and tt == 0 and oc == 0:
                                    for s_, v_ in first_y_guard:
                                        EW("v", s_, v_)
                                wcol = wrt[:, tt * NE + ex:tt * NE + ex + 1]
                                sl = slice(tt * D + oc * 512, tt * D + oc * 512 + 512)
                                if ex == 0 and S == 0:
                                    yv = EI("v", lambda e, sl=sl, pi=pi, wcol=wcol:
                                            e.tensor_scalar_mul(
                                                xa[:, sl], ps[pi][:, :], wcol), "VV", 1)
                                else:
                                    yv = EI("v", lambda e, sl=sl, pi=pi, wcol=wcol:
                                            e.scalar_tensor_tensor(
                                                xa[:, sl], ps[pi][:, :], wcol,
                                                xa[:, sl], ALU.mult, ALU.add), "VV", 1)
                                psMR(pi, "VV", yv)
                        y_mm_s[S] = cnt["MM"]
                        for B2 in (2 * S + 2, 2 * S + 3):
                            if B2 < NBLK:
                                load_w2_m(B2, y_mm_s[S])
                    b2_ss_mm = y_mm_s[0]
                acc_vv = cnt["VV"]

                ld_x0 = [0, 0]
                ld_st = [0, 0]
                used = [0, 0]
                EW("y", "MM", cnt["MM"])
                EW("y", "SP", s_spill_dx)
                for i in range(2):
                    ld_x0[i] = EI("y", lambda e, i=i: e.dma_start(
                        out=x0b[i], in_=x_in[i * 128:(i + 1) * 128, :]), "F%d" % i, 16)
                    ld_st[i] = EI("y", lambda e, i=i: e.dma_start(
                        out=stb[i], in_=s_dram[i * 128:(i + 1) * 128, :]), "F%d" % i, 16)
                out_dx = 0
                prev_uvv = 0
                for i in range(NT):
                    j = i % 2
                    EW("s", "VV", max(acc_vv, prev_uvv))
                    sv = EI("s", lambda e, i=i: e.activation(
                        gbuf, xa[:, i * D:(i + 1) * D], AF.Silu), "SS", 1)
                    EW("v", "SS", sv)
                    EW("v", "F%d" % j, max(ld_x0[j], ld_st[j]))
                    EI("v", lambda e, j=j: e.tensor_mul(
                        gbuf, gbuf, stb[j]), "VV", 1)
                    uvv = EI("v", lambda e, i=i, j=j: e.tensor_add(
                        xa[:, i * D:(i + 1) * D], gbuf, x0b[j]), "VV", 1)
                    used[j] = uvv
                    prev_uvv = uvv
                    EW("y", "VV", uvv)
                    out_dx = EI("y", lambda e, i=i: e.dma_start(
                        out=out_d[i * 128:(i + 1) * 128, :],
                        in_=xa[:, i * D:(i + 1) * D]), "OUT", 16)
                    if i + 2 < NT:
                        EW("y", "VV", used[j])
                        ld_x0[j] = EI("y", lambda e, i=i, j=j: e.dma_start(
                            out=x0b[j], in_=x_in[(i + 2) * 128:(i + 3) * 128, :]),
                            "F%d" % j, 16)
                        ld_st[j] = EI("y", lambda e, i=i, j=j: e.dma_start(
                            out=stb[j], in_=s_dram[(i + 2) * 128:(i + 3) * 128, :]),
                            "F%d" % j, 16)

        except StopEmit:
            pass

        EW("y", "OUT", cnt["OUT"])
        EW("y", "CC", cnt["CC"])

        block = ctx.enter_context(nc.Block())

        @block.tensor
        def _(e):
            for f in streams["t"]:
                f(e)

        @block.vector
        def _(e):
            for f in streams["v"]:
                f(e)

        @block.scalar
        def _(e):
            for f in streams["s"]:
                f(e)

        @block.gpsimd
        def _(e):
            for f in streams["g"]:
                f(e)

        @block.sync
        def _(e):
            for f in streams["y"]:
                f(e)

    return nc


def make_in_maps(inputs, depth=6):
    f = np.ascontiguousarray
    x = np.asarray(inputs["L0_attn_x"], np.float32)
    com = {
        "wq": f(np.asarray(inputs["L0_attn_wq"], np.float32)),
        "wk": f(np.asarray(inputs["L0_attn_wk"], np.float32)),
        "wv": f(np.asarray(inputs["L0_attn_wv"], np.float32)),
        "wo": f(np.asarray(inputs["L0_attn_wo"], np.float32)),
        "w1": f(np.asarray(inputs["L0_attn_w1"], np.float32)),
        "w2": f(np.asarray(inputs["L0_attn_w2"], np.float32)),
        "ln1w": f(np.asarray(inputs["L0_attn_ln1_w"], np.float32)),
        "ln1b": f(np.asarray(inputs["L0_attn_ln1_b"], np.float32)),
        "ln2w": f(np.asarray(inputs["L0_attn_ln2_w"], np.float32)),
        "ln2b": f(np.asarray(inputs["L0_attn_ln2_b"], np.float32)),
        "bo": f(np.asarray(inputs["L0_attn_bo"], np.float32)),
        "wr": f(np.asarray(inputs["L0_ffn_W_router"], np.float32)),
        "br": f(np.asarray(inputs["L0_ffn_b_router"], np.float32).reshape(1, NE)),
        "W1m": f(np.asarray(inputs["L0_ffn_W1"], np.float32)),
        "b1m": f(np.asarray(inputs["L0_ffn_b1"], np.float32)),
        "W2m": f(np.asarray(inputs["L0_ffn_W2"], np.float32)),
        "b2m": f(np.asarray(inputs["L0_ffn_b2"], np.float32)),
        "ident": np.eye(128, dtype=np.float32),
    }
    maps = []
    for c in range(8):
        b, hh = c // 2, c % 2
        m = dict(com)
        m["x_in"] = f(x[b, hh * T:(hh + 1) * T, :])
        maps.append(m)
    return maps


def kernel(**inputs):
    nc = build_program(depth=6, do_moe=True)
    maps = make_in_maps(inputs)
    res = run_bass_kernel_spmd(nc, maps, core_ids=list(range(8)))
    out = np.empty((4, 4096, 1024), np.float32)
    for c in range(8):
        b, hh = c // 2, c % 2
        out[b, hh * T:(hh + 1) * T, :] = np.asarray(res.results[c]["out"])
    return out


# version 12
# speedup vs baseline: 1.0885x; 1.0885x over previous
import numpy as np
import ml_dtypes
from contextlib import ExitStack

import concourse.bass as bass
import concourse.mybir as mybir
from concourse.bass_utils import run_bass_kernel_spmd

F32 = mybir.dt.float32
BF16 = mybir.dt.bfloat16
AF = mybir.ActivationFunctionType
ALU = mybir.AluOpType
AX = mybir.AxisListType

D = 1024
T = 2048
NT = 16
NDT = 8
H = 8
DH = 128
FF = 4096
NFT = 32
NE = 8
K0 = 4.0
EPS = 1e-5
NCH = 4


def build_program(depth=6, do_moe=True, dbg=(), use_lrelu=True, stop_stage=None):
    nc = bass.Bass("TRN2", target_bir_lowering=False)

    x_in = nc.dram_tensor("x_in", [T, D], F32, kind="ExternalInput")
    wq_d = nc.dram_tensor("wq", [6, D, D], F32, kind="ExternalInput")
    wk_d = nc.dram_tensor("wk", [6, D, D], F32, kind="ExternalInput")
    wv_d = nc.dram_tensor("wv", [6, D, D], F32, kind="ExternalInput")
    wo_d = nc.dram_tensor("wo", [6, D, D], F32, kind="ExternalInput")
    w1_d = nc.dram_tensor("w1", [6, D, FF], F32, kind="ExternalInput")
    w2_d = nc.dram_tensor("w2", [6, FF, D], F32, kind="ExternalInput")
    ln1w_d = nc.dram_tensor("ln1w", [6, D], F32, kind="ExternalInput")
    ln1b_d = nc.dram_tensor("ln1b", [6, D], F32, kind="ExternalInput")
    ln2w_d = nc.dram_tensor("ln2w", [6, D], F32, kind="ExternalInput")
    ln2b_d = nc.dram_tensor("ln2b", [6, D], F32, kind="ExternalInput")
    bo_d = nc.dram_tensor("bo", [6, D], F32, kind="ExternalInput")
    wr_d = nc.dram_tensor("wr", [D, NE], F32, kind="ExternalInput")
    br_d = nc.dram_tensor("br", [1, NE], F32, kind="ExternalInput")
    W1m_d = nc.dram_tensor("W1m", [NE, D, FF], F32, kind="ExternalInput")
    b1m_d = nc.dram_tensor("b1m", [NE, FF], F32, kind="ExternalInput")
    W2m_d = nc.dram_tensor("W2m", [NE, FF, D], F32, kind="ExternalInput")
    b2m_d = nc.dram_tensor("b2m", [NE, D], F32, kind="ExternalInput")
    ident_d = nc.dram_tensor("ident", [128, 128], F32, kind="ExternalInput")
    out_d = nc.dram_tensor("out", [T, D], F32, kind="ExternalOutput")
    cc_in = nc.dram_tensor("cc_in", [128, 1032], F32)
    cc_out = nc.dram_tensor("cc_out", [128, 1032], F32)
    xn0_dram = nc.dram_tensor("xn0_dram", [128, NDT * T], BF16)
    s_dram = nc.dram_tensor("s_dram", [T, D], F32)
    dbg_t = {}
    for name in dbg:
        shp = [128, NT * NE] if name == "wrt" else [128, NDT * T]
        dbg_t[name] = nc.dram_tensor("dbg_" + name, shp, F32, kind="ExternalOutput")
    stop_d = None
    if stop_stage is not None:
        stop_d = nc.dram_tensor("stopdump", [128, NDT * T], F32, kind="ExternalOutput")

    class StopEmit(Exception):
        pass

    with ExitStack() as ctx:
        sb = lambda n, s, d: ctx.enter_context(nc.sbuf_tensor(n, s, d))
        xa = sb("xa", [128, NDT * T], F32)
        xnT = sb("xnT", [128, NDT * T], BF16)
        eqT = sb("eqT", [128, H * T], BF16)
        hs = [sb("hs%d" % i, [128, 4096], BF16) for i in range(4)]
        hs4 = sb("hs4", [128, 8192], BF16)
        ek = [sb("ek%d" % i, [128, D], BF16) for i in range(2)]
        vt = [sb("vt%d" % i, [128, H * 129], BF16) for i in range(2)]
        b2row = vt[0][0:1, 0:D]
        spkg = sb("spkg", [128, 1032], F32)
        ctxbf = sb("ctxbf", [128, H * DH], BF16)
        aps1 = sb("aps1", [128, 512], F32)
        aps2 = sb("aps2", [128, 512], F32)
        actx = aps2
        arow = sb("arow", [1, 512], F32)
        mrow_f = sb("mrow_f", [1, 512], F32)
        mrow_b = sb("mrow_b", [1, 512], BF16)
        rrow_b = sb("rrow_b", [1, 512], BF16)
        vrow = sb("vrow", [1, 512], F32)
        vrow2 = None
        rqrow = None
        lnrows = hs4[:, :].bitcast(F32)[0:32, 0:D]
        lncols = sb("lncols", [128, NDT * 32], F32)
        b1c = sb("b1c", [128, NFT * NE], F32)
        brrow = sb("brrow", [1, NE], BF16)
        wrb = sb("wrb", [128, NDT * NE], BF16)
        wrt = sb("wrt", [128, NT * NE], F32)
        lgt = sb("lgt", [128, NT * NE], F32)
        rcol = [sb("rc%d" % i, [128, 1], F32) for i in range(4)]
        onesrow = sb("onesrow", [1, 512], BF16)
        onescol = sb("onescol", [128, 1], BF16)
        negk0c = sb("negk0c", [128, 1], F32)
        epscol = sb("epscol", [128, 1], F32)
        identf = sb("identf", [128, 128], F32)
        eqf = eqT[:, :].bitcast(F32)
        x0b = [eqf[:, i * D:(i + 1) * D] for i in range(2)]
        stb = [eqf[:, (2 + i) * D:(3 + i) * D] for i in range(2)]
        gbuf = eqf[:, 4 * D:5 * D]
        mf_B = eqf[0:1, 5120:5632]
        v1_B = eqf[0:1, 5632:6144]
        m2_B = eqf[0:1, 6144:6656]
        mb_B = eqT[0:1, 13312:13824]
        rr_B = eqT[0:1, 13824:14336]
        xsq = [eqT[:, 14336:14848], eqT[:, 14848:15360]]
        lnm_sb = [gbuf[:, 0:512], ctxbf[:, :].bitcast(F32)[:, 0:512]]
        lnr_sb = [gbuf[:, 512:1024], eqf[:, 7680:8192]]
        m2_A = None
        ps = [ctx.enter_context(nc.psum_tensor("ps%d" % i, [128, 512], F32))
              for i in range(8)]

        semh, cnt = {}, {}
        for s in ["MM", "VV", "SS", "GG", "CC", "W0", "W1", "W2", "W3",
                  "W4", "W5", "GC", "CD", "B2", "YC", "F0", "F1", "SP", "OUT"]:
            semh[s] = ctx.enter_context(nc.semaphore(s))
            cnt[s] = 0
        streams = {k: [] for k in "tvsgy"}

        def E(eng, fn):
            streams[eng].append(fn)

        def EI(eng, fn, sem, k=1):
            cnt[sem] += k
            E(eng, lambda e, fn=fn, s=semh[sem], k=k: fn(e).then_inc(s, k))
            return cnt[sem]

        def EW(eng, sem, v):
            if v > 0:
                E(eng, lambda e, s=semh[sem], v=v: e.wait_ge(s, v))

        def VW(fn):
            v = EI("v", fn, "VV", 1)
            EW("v", "VV", v)
            return v

        ps_w = [("MM", 0)] * 8
        ps_r = [[] for _ in range(8)]

        def psW(i, eng="t"):
            for s, v in ps_r[i]:
                EW(eng, s, v)
            ps_r[i] = []

        def psMW(i, val):
            ps_w[i] = ("MM", val)

        def psR(i, eng):
            s, v = ps_w[i]
            EW(eng, s, v)

        def psMR(i, sem, val):
            ps_r[i].append((sem, val))

        def load_mat_1024(slots, sems, src3, l, wait_mm=0):
            if wait_mm:
                EW("g", "MM", wait_mm)
            rdy = []
            for j in range(2):
                src = src3[l, j * 512:(j + 1) * 512, :].rearrange(
                    "(kt p) m -> p kt m", p=128)
                sl = slots[j] if isinstance(slots[j], bass.AP) else slots[j][:, :]
                dst = sl.rearrange("p (kt m) -> p kt m", m=1024)
                v = EI("g", lambda e, d=dst, s=src:
                       e.dma_start(out=d, in_=s), sems[j], 16)
                rdy.append((sems[j], v))
            return rdy

        def EWL(eng, lst):
            for s_, v_ in lst:
                EW(eng, s_, v_)

        def maybe_stop(stage, src_ap, waits):
            if stop_stage != stage:
                return
            for s_, v_ in waits:
                EW("g", s_, v_)
            npart = src_ap.ap[0][1]
            nfree = src_ap.ap[1][1]
            EI("g", lambda e: e.dma_start(
                out=stop_d[0:npart, 0:nfree], in_=src_ap), "OUT", 16)
            raise StopEmit()

        EI("y", lambda e: e.dma_start(out=identf[:, :], in_=ident_d[:, :]), "YC", 16)
        EW("y", "GG", 1)
        for row, src in ((0, ln1w_d), (6, ln1b_d), (12, ln2w_d), (18, ln2b_d), (24, bo_d)):
            EI("y", lambda e, r=row, s=src: e.dma_start(
                out=lnrows[r:r + 6, :], in_=s[:, :]), "YC", 16)
        EI("y", lambda e: e.dma_start(out=xa[0:8, 0:FF], in_=b1m_d[:, :]), "YC", 16)
        c_const_y = cnt["YC"]
        c_lnz = EI("g", lambda e: e.memset(lnrows[0:32, :], 0.0), "GG", 1)
        EI("g", lambda e: e.dma_start(
            out=wrb[:, :].rearrange("p (kt n) -> p kt n", n=NE),
            in_=wr_d.rearrange("(kt p) n -> p kt n", p=128)), "GC", 16)
        EI("g", lambda e: e.dma_start(out=brrow[:, :], in_=br_d[:, :]), "GC", 16)
        c_const_g = cnt["GC"]
        E("g", lambda e: e.memset(onesrow[:, :], 1.0))
        E("g", lambda e: e.memset(onescol[:, :], 1.0))
        E("g", lambda e: e.memset(negk0c[:, :], -K0))
        E("g", lambda e: e.memset(epscol[:, :], EPS))

        for b in range(2):
            for h in range(H):
                E("g", lambda e, b=b, h=h: e.memset(
                    vt[b][:, h * 129 + 128:h * 129 + 129], 1.0))
        c_memset = EI("g", lambda e: e.engine_nop(), "GG", 1)

        EW("t", "YC", c_const_y)
        EW("t", "GG", c_memset)
        for dt in range(NDT):
            pi = dt % 4
            psW(pi)
            mv = EI("t", lambda e, dt=dt, pi=pi: e.transpose(
                ps[pi][:, 0:32], lnrows[:, dt * 128:(dt + 1) * 128],
                identf[0:32, 0:32]), "MM", 1)
            psMW(pi, mv)
            psR(pi, "s")
            sv = EI("s", lambda e, dt=dt, pi=pi: e.copy(
                lncols[:, dt * 32:(dt + 1) * 32], ps[pi][:, 0:32]), "SS", 1)
            psMR(pi, "SS", sv)
        c_lnrows_mm = cnt["MM"]
        for ft in range(NFT):
            pi = ft % 4
            psW(pi)
            mv = EI("t", lambda e, ft=ft, pi=pi: e.transpose(
                ps[pi][:, 0:8], xa[0:8, ft * 128:(ft + 1) * 128],
                identf[0:8, 0:8]), "MM", 1)
            psMW(pi, mv)
            psR(pi, "v")
            vv = EI("v", lambda e, ft=ft, pi=pi: e.tensor_copy(
                b1c[:, ft * NE:(ft + 1) * NE], ps[pi][:, 0:8]), "VV", 1)
            psMR(pi, "VV", vv)
        c_b1c_vv = cnt["VV"]

        x_load = [0] * NT
        for i in range(2):
            x_load[i] = EI("y", lambda e, i=i: e.dma_start(
                out=x0b[i], in_=x_in[i * 128:(i + 1) * 128, :]), "F%d" % i, 16)
        xa_p0_ss = 0
        gb_rd_vv = 0
        tr_done = [0, 0]
        EW("s", "GG", c_memset)
        for i in range(NT):
            j = i % 2
            if i == 0:
                EW("s", "VV", c_b1c_vv)
            EW("v", "F%d" % j, x_load[i])
            VW(lambda e, j=j: e.reduce_sum(rcol[0][:, :], x0b[j], axis=AX.X))
            EW("s", "F%d" % j, x_load[i])
            EW("s", "VV", gb_rd_vv)
            sv1 = EI("s", lambda e, j=j: e.activation(
                gbuf, x0b[j], AF.Square), "SS", 1)
            EW("v", "SS", sv1)
            gb_rd_vv = VW(lambda e: e.reduce_sum(rcol[1][:, :], gbuf, axis=AX.X))
            VW(lambda e: e.tensor_scalar_mul(rcol[2][:, :], rcol[0][:, :], 1.0 / D))
            VW(lambda e: e.tensor_scalar_mul(rcol[3][:, :], rcol[1][:, :], 1.0 / D))
            VW(lambda e: e.tensor_mul(rcol[0][:, :], rcol[2][:, :], rcol[2][:, :]))
            vv2 = VW(lambda e: e.tensor_sub(rcol[1][:, :], rcol[3][:, :], rcol[0][:, :]))
            EW("s", "VV", vv2)
            EI("s", lambda e: e.activation(rcol[3][:, :], rcol[1][:, :], AF.Ln, bias=epscol[:, :]), "SS", 1)
            sv2 = EI("s", lambda e: e.activation(rcol[1][:, :], rcol[3][:, :], AF.Exp, scale=-0.5), "SS", 1)
            EW("v", "SS", sv2)
            VW(lambda e: e.tensor_mul(rcol[0][:, :], rcol[2][:, :], rcol[1][:, :]))
            VW(lambda e: e.tensor_scalar_mul(rcol[2][:, :], rcol[0][:, :], -1.0))
            if i >= 2:
                EW("v", "MM", tr_done[i % 2])
            vv3 = EI("v", lambda e, j=j: e.tensor_scalar(
                stb[j], x0b[j], rcol[1][:, :], rcol[2][:, :],
                ALU.mult, ALU.add), "VV", 1)
            if i + 2 < NT:
                EW("y", "VV", vv3)
                x_load[i + 2] = EI("y", lambda e, i=i: e.dma_start(
                    out=x0b[i % 2][:, :],
                    in_=x_in[(i + 2) * 128:(i + 3) * 128, :]), "F%d" % (i % 2), 16)
            for dt in range(NDT):
                pi = dt % 4
                psW(pi)
                EW("t", "VV", vv3)
                mv = EI("t", lambda e, dt=dt, pi=pi, j=j: e.transpose(
                    ps[pi][:, 0:128], stb[j][:, dt * 128:(dt + 1) * 128],
                    identf[:, :]), "MM", 1)
                psMW(pi, mv)
                psR(pi, "s")
                xa_p0_ss = EI("s", lambda e, i=i, dt=dt, pi=pi: e.copy(
                    xa[:, dt * T + i * 128:dt * T + (i + 1) * 128],
                    ps[pi][:, 0:128]), "SS", 1)
                psMR(pi, "SS", xa_p0_ss)
                EW("v", "SS", xa_p0_ss)
                vv4 = EI("v", lambda e, i=i, dt=dt, pi=pi: e.tensor_copy(
                    xnT[:, dt * T + i * 128:dt * T + (i + 1) * 128],
                    ps[pi][:, 0:128]), "VV", 1)
                psMR(pi, "VV", vv4)
            tr_done[i % 2] = cnt["MM"]
        c_ln0_ss = cnt["SS"]
        c_ln0_vv = cnt["VV"]

        EW("y", "VV", c_ln0_vv)
        c_xn0_spill = EI("y", lambda e: e.dma_start(
            out=xn0_dram[:, :], in_=xnT[:, :]), "SP", 16)

        EW("t", "GC", c_const_g)
        EW("t", "GG", c_memset)
        lgt_ss = 0
        for i in range(NT):
            pi = i % 2
            psW(pi)
            for dt in range(NDT):
                E("t", lambda e, i=i, dt=dt, pi=pi: e.matmul(
                    ps[pi][:, 0:NE],
                    xnT[:, dt * T + i * 128:dt * T + (i + 1) * 128],
                    wrb[:, dt * NE:(dt + 1) * NE],
                    start=(dt == 0), stop=False))
            mv = EI("t", lambda e, pi=pi: e.matmul(
                ps[pi][:, 0:NE], onesrow[0:1, 0:128], brrow[0:1, :],
                start=False, stop=True), "MM", 1)
            psMW(pi, mv)
            psR(pi, "s")
            lgt_ss = EI("s", lambda e, i=i, pi=pi: e.copy(
                lgt[:, i * NE:(i + 1) * NE], ps[pi][:, 0:NE]), "SS", 1)
            psMR(pi, "SS", lgt_ss)
        BIG = 1e30
        EW("v", "SS", lgt_ss)
        rr = [gbuf[:, 128 + k * 16:128 + (k + 1) * 16] for k in range(4)]

        def step(fn, dep=True):
            for i in range(NT):
                L = lgt[:, i * NE:(i + 1) * NE]
                W = wrt[:, i * NE:(i + 1) * NE]
                EI("v", lambda e, i=i, L=L, W=W: fn(e, i, L, W), "VV", 1)
            v = cnt["VV"]
            EW("v", "VV", v)
            return v

        step(lambda e, i, L, W: e.reduce_max(rr[0][:, i:i + 1], L, axis=AX.X, negate=True))
        step(lambda e, i, L, W: e.tensor_scalar(
            W, L, rr[0][:, i:i + 1], 0.0, ALU.add, ALU.is_equal))
        step(lambda e, i, L, W: e.tensor_scalar_mul(W, W, -BIG))
        step(lambda e, i, L, W: e.tensor_add(W, W, L))
        step(lambda e, i, L, W: e.reduce_max(rr[1][:, i:i + 1], W, axis=AX.X, negate=True))
        vsel = step(lambda e, i, L, W: e.tensor_scalar(
            W, L, rr[1][:, i:i + 1], 0.0, ALU.add, ALU.is_ge))
        EW("s", "VV", vsel)
        sv = 0
        for i in range(NT):
            L = lgt[:, i * NE:(i + 1) * NE]
            sv = EI("s", lambda e, i=i, L=L: e.activation(
                gbuf[:, i * NE:(i + 1) * NE], L, AF.Exp,
                bias=rr[0][:, i:i + 1]), "SS", 1)
        EW("v", "SS", sv)
        step(lambda e, i, L, W: e.tensor_mul(W, W, gbuf[:, i * NE:(i + 1) * NE]))
        step(lambda e, i, L, W: e.reduce_sum(rr[2][:, i:i + 1], W, axis=AX.X))
        step(lambda e, i, L, W: e.reciprocal(rr[3][:, i:i + 1], rr[2][:, i:i + 1]))
        step(lambda e, i, L, W: e.tensor_scalar_mul(W, W, rr[3][:, i:i + 1]))
        c_router_vv = cnt["VV"]

        wq_rdy = load_mat_1024([hs4[:, 0:4096], hs4[:, 4096:8192]],
                               ("W4", "W5"), wq_d, 0, wait_mm=c_lnrows_mm)
        wk_rdy = load_mat_1024(hs[2:4], ("W2", "W3"), wk_d, 0)
        wv_rdy0 = load_mat_1024(hs[0:2], ("W0", "W1"), wv_d, 0)

        def emit_ln(l, which, waits):
            wofs, bofs = (0, 6) if which == 0 else (12, 18)
            rset = [
                dict(mf=mrow_f[:, :], mb=mrow_b[:, :], rr=rrow_b[:, :],
                     v1=vrow[:, :], m2=arow[:, :]),
                dict(mf=mf_B, mb=mb_B, rr=rr_B, v1=v1_B, m2=m2_B),
            ]
            sums_bank = [(6, 7), (2, 3)]
            bc_bank = [(4, 5), (0, 1)]
            apply_vv = [0] * NCH
            mm_sums = [0] * NCH
            rows_ss = [0] * NCH
            first = [True]

            def emit_sums(c):
                b0, b1 = sums_bank[c % 2]
                psW(b0)
                psW(b1)
                sq_mm = {}
                for dt in range(NDT):
                    sl = slice(dt * T + c * 512, dt * T + c * 512 + 512)
                    if first[0]:
                        for s_, v_ in waits:
                            EW("s", s_, v_)
                        first[0] = False
                    vv = EI("s", lambda e, sl=sl: e.copy(
                        xnT[:, sl], xa[:, sl]), "SS", 1)
                    if dt >= 2:
                        EW("s", "MM", sq_mm[dt - 2])
                    sv = EI("s", lambda e, sl=sl, dt=dt: e.activation(
                        xsq[dt % 2][:, :], xnT[:, sl], AF.Square), "SS", 1)
                    EW("t", "SS", vv)
                    E("t", lambda e, sl=sl, dt=dt, b0=b0: e.matmul(
                        ps[b0][0:1, :], onescol[:, :], xnT[:, sl],
                        start=(dt == 0), stop=(dt == NDT - 1),
                        skip_group_check=True))
                    EW("t", "SS", sv)
                    sq_mm[dt] = EI("t", lambda e, dt=dt, b1=b1: e.matmul(
                        ps[b1][0:1, :], onescol[:, :], xsq[dt % 2][:, :],
                        start=(dt == 0), stop=(dt == NDT - 1),
                        skip_group_check=True), "MM", 1)
                mm_sums[c] = cnt["MM"]
                psMW(b0, mm_sums[c])
                psMW(b1, mm_sums[c])

            def emit_rows(c):
                b0, b1 = sums_bank[c % 2]
                r = rset[c % 2]
                psR(b0, "s")
                psR(b1, "s")
                EI("s", lambda e, r=r, b0=b0: e.activation(
                    r["mf"], ps[b0][0:1, :], AF.Copy, scale=1.0 / D), "SS", 1)
                EI("s", lambda e, r=r, b0=b0: e.activation(
                    r["mb"], ps[b0][0:1, :], AF.Copy, scale=1.0 / D), "SS", 1)
                EI("s", lambda e, r=r, b1=b1: e.activation(
                    r["v1"], ps[b1][0:1, :], AF.Copy, scale=1.0 / D), "SS", 1)
                sv2 = EI("s", lambda e, r=r: e.activation(
                    r["m2"], r["mf"], AF.Square), "SS", 1)
                psMR(b0, "SS", sv2)
                psMR(b1, "SS", sv2)
                EW("g", "SS", sv2)
                vv2 = EI("g", lambda e, r=r: e.tensor_sub(r["v1"], r["v1"], r["m2"]), "GG", 1)
                EW("s", "GG", vv2)
                EI("s", lambda e, r=r: e.activation(
                    r["v1"], r["v1"], AF.Ln, bias=epscol[0:1, :]), "SS", 1)
                rows_ss[c] = EI("s", lambda e, r=r: e.activation(
                    r["rr"], r["v1"], AF.Exp, scale=-0.5), "SS", 1)

            def emit_bcast_apply(c):
                r = rset[c % 2]
                p0, p1 = bc_bank[c % 2]
                psW(p0)
                psW(p1)
                EW("t", "SS", rows_ss[c])
                E("t", lambda e, r=r, p0=p0: e.matmul(
                    ps[p0][:, :], onesrow[0:1, 0:128], r["mb"],
                    start=True, stop=True))
                mmb = EI("t", lambda e, r=r, p1=p1: e.matmul(
                    ps[p1][:, :], onesrow[0:1, 0:128], r["rr"],
                    start=True, stop=True), "MM", 1)
                psMW(p0, mmb)
                psMW(p1, mmb)
                psR(p0, "s")
                psR(p1, "s")
                mb_sb = lnm_sb[c % 2]
                rb_sb = lnr_sb[c % 2]
                if c >= 2:
                    EW("s", "VV", apply_vv[c - 2])
                EI("s", lambda e, p0=p0, mb_sb=mb_sb: e.copy(
                    mb_sb, ps[p0][:, :]), "SS", 1)
                se2 = EI("s", lambda e, p1=p1, rb_sb=rb_sb: e.copy(
                    rb_sb, ps[p1][:, :]), "SS", 1)
                psMR(p0, "SS", se2)
                psMR(p1, "SS", se2)
                EW("v", "SS", se2)
                vv5 = 0
                for dt in range(NDT):
                    sl = slice(dt * T + c * 512, dt * T + c * 512 + 512)
                    EI("v", lambda e, sl=sl, mb_sb=mb_sb: e.tensor_sub(
                        aps1[:, :], xa[:, sl], mb_sb), "VV", 1)
                    EI("v", lambda e, rb_sb=rb_sb: e.tensor_mul(
                        aps2[:, :], aps1[:, :], rb_sb), "VV", 1)
                    wc = lncols[:, dt * 32 + wofs + l:dt * 32 + wofs + l + 1]
                    bc = lncols[:, dt * 32 + bofs + l:dt * 32 + bofs + l + 1]
                    vv5 = EI("v", lambda e, sl=sl, wc=wc, bc=bc: e.tensor_scalar(
                        xnT[:, sl], aps2[:, :], wc, bc, ALU.mult, ALU.add), "VV", 1)
                apply_vv[c] = vv5

            for c in range(NCH):
                emit_sums(c)
                if c >= 1:
                    emit_rows(c - 1)
                    emit_bcast_apply(c - 1)
            emit_rows(NCH - 1)
            emit_bcast_apply(NCH - 1)
            return apply_vv

        def emit_ffn(l, ln_avv, free_mm_hs01, free_mm_hs23):
            NBLK = NFT // 4
            w1_rdy = [0] * NBLK
            w2_rdy = [0] * NBLK
            h_mm = [0] * NBLK
            y_mm = [0] * (NBLK // 2)

            def w1sl(B, fl):
                return hs[B % 2][:, fl * 1024:(fl + 1) * 1024]

            def w2sl(B, fl):
                return hs[2 + B % 2][:, fl * 1024:(fl + 1) * 1024]

            def load_blk_w1(B, wmm):
                s1 = "W%d" % (B % 2)
                if wmm:
                    EW("g", "MM", wmm)
                for fl in range(4):
                    ft = B * 4 + fl
                    src = w1_d[l, :, ft * 128:(ft + 1) * 128].rearrange(
                        "(kt p) m -> p kt m", p=128)
                    dst = w1sl(B, fl).rearrange("p (kt m) -> p kt m", m=128)
                    EI("g", lambda e, d=dst, s=src:
                       e.dma_start(out=d, in_=s), s1, 16)
                w1_rdy[B] = (s1, cnt[s1])

            def load_blk_w2(B, wmm2):
                s2 = "W%d" % (2 + B % 2)
                if wmm2:
                    EW("g", "MM", wmm2)
                for fl in range(4):
                    ft = B * 4 + fl
                    src = w2_d[l, ft * 128:(ft + 1) * 128, :]
                    EI("g", lambda e, d=w2sl(B, fl), s=src:
                       e.dma_start(out=d, in_=s), s2, 16)
                w2_rdy[B] = (s2, cnt[s2])

            load_blk_w1(0, free_mm_hs01)
            load_blk_w2(0, free_mm_hs23)
            load_blk_w1(1, free_mm_hs01)
            load_blk_w2(1, free_mm_hs23)
            y_vv = 0
            relu_ss = {}
            lrelu_vv = [0]
            relu_sem = "SS" if use_lrelu else "VV"
            for S in range(NBLK // 2):
                for B in (2 * S, 2 * S + 1):
                    EW("t", *w1_rdy[B])
                    half = (B % 2) * (H * T // 2)
                    for fl in range(4):
                        for c in range(NCH):
                            if B == 0 and fl == 0:
                                EW("t", "VV", ln_avv[c] if use_lrelu else ln_avv[-1])
                            pi = 4 + (fl * NCH + c) % 2
                            psW(pi)
                            for dt in range(NDT - 1):
                                E("t", lambda e, B=B, fl=fl, c=c, dt=dt, pi=pi: e.matmul(
                                    ps[pi][:, :], w1sl(B, fl)[:, dt * 128:(dt + 1) * 128],
                                    xnT[:, dt * T + c * 512:dt * T + c * 512 + 512],
                                    start=(dt == 0), stop=False))
                            mv = EI("t", lambda e, B=B, fl=fl, c=c, pi=pi: e.matmul(
                                ps[pi][:, :], w1sl(B, fl)[:, 7 * 128:8 * 128],
                                xnT[:, 7 * T + c * 512:7 * T + c * 512 + 512],
                                start=False, stop=True), "MM", 1)
                            psMW(pi, mv)
                            hsl = slice(half + fl * T + c * 512,
                                        half + fl * T + c * 512 + 512)
                            if use_lrelu:
                                psR(pi, "s")
                                sv = EI("s", lambda e, hsl=hsl, pi=pi:
                                        e.activation(eqT[:, hsl], ps[pi][:, :],
                                                     AF.Lrelu, alpha=0.01), "SS", 1)
                                psMR(pi, "SS", sv)
                            else:
                                psR(pi, "s")
                                EW("s", "VV", lrelu_vv[0])
                                s1 = EI("s", lambda e, pi=pi: e.activation(
                                    aps1[:, :], ps[pi][:, :], AF.Copy,
                                    scale=0.01), "SS", 1)
                                psMR(pi, "SS", s1)
                                psR(pi, "v")
                                EW("v", "SS", s1)
                                sv = EI("v", lambda e, hsl=hsl, pi=pi:
                                        e.tensor_max(eqT[:, hsl], ps[pi][:, :],
                                                     aps1[:, :]), "VV", 1)
                                psMR(pi, "VV", sv)
                                lrelu_vv[0] = sv
                            relu_ss[(B, fl, c)] = sv
                    h_mm[B] = cnt["MM"]
                    if B == 1:
                        maybe_stop("h01", eqT[:, :], [(relu_sem,
                                    relu_ss[(1, 3, NCH - 1)])])
                    if B + 2 < NBLK:
                        load_blk_w1(B + 2, h_mm[B])
                EW("t", *w2_rdy[2 * S])
                EW("t", *w2_rdy[2 * S + 1])
                EW("t", relu_sem, relu_ss[(2 * S + 1, 3, NCH - 1)])
                for ot in range(NDT):
                    for c in range(NCH):
                        pi = (ot * NCH + c) % 4
                        psW(pi)
                        n = 0
                        for B in (2 * S, 2 * S + 1):
                            half = (B % 2) * (H * T // 2)
                            for fl in range(4):
                                emitfn = (lambda e, B=B, fl=fl, ot=ot, c=c, pi=pi,
                                          half=half, n=n: e.matmul(
                                    ps[pi][:, :],
                                    w2sl(B, fl)[:, ot * 128:(ot + 1) * 128],
                                    eqT[:, half + fl * T + c * 512:
                                        half + fl * T + c * 512 + 512],
                                    start=(n == 0), stop=(n == 7)))
                                if n == 7:
                                    mv = EI("t", emitfn, "MM", 1)
                                else:
                                    E("t", emitfn)
                                n += 1
                        psMW(pi, mv)
                        psR(pi, "v")
                        y_vv = EI("v", lambda e, ot=ot, c=c, pi=pi: e.tensor_add(
                            xa[:, ot * T + c * 512:ot * T + c * 512 + 512],
                            xa[:, ot * T + c * 512:ot * T + c * 512 + 512],
                            ps[pi][:, :]), "VV", 1)
                        psMR(pi, "VV", y_vv)
                y_mm[S] = cnt["MM"]
                for B2 in (2 * S + 2, 2 * S + 3):
                    if B2 < NBLK:
                        load_blk_w2(B2, y_mm[S])
                if S == 0:
                    maybe_stop("y0", xa[:, :], [("VV", y_vv)])
            return y_vv, h_mm[NBLK - 1], y_mm[-1]

        xnT_free = 0
        spkg_rd_v = 0
        xa_ready = [("SS", c_ln0_ss), ("VV", c_router_vv)]
        eqT_free_mm = 0
        wq_ready, wk_ready, wv_ready = wq_rdy, wk_rdy, wv_rdy0
        dbg_done = []

        try:
            maybe_stop("p0x", xa[:, :], [("SS", c_ln0_ss)])
            for l in range(depth):
                ln1_avv = emit_ln(l, 0, xa_ready + [("MM", xnT_free)])
                ln1_vv = ln1_avv[-1]
                maybe_stop("ln1", xnT[:, :], [("VV", ln1_vv)])

                EWL("t", wk_ready)
                psW(5)
                psW(6)
                psW(7)
                ek_ss = [0, 0]
                vt_vv = [0, 0]
                ekv_rd = [0, 0]
                for i in range(NT):
                    j = i % 2
                    if i % 4 == 0:
                        EW("t", "VV", ln1_avv[i // 4])
                    for nch in range(2):
                        pi = nch
                        psW(pi)
                        for dt in range(NDT):
                            wsl = hs[2 + dt // 4][:, (dt % 4) * 1024 + nch * 512:
                                                  (dt % 4) * 1024 + nch * 512 + 512]
                            emitfn = (lambda e, i=i, dt=dt, pi=pi, wsl=wsl: e.matmul(
                                ps[pi][:, :],
                                xnT[:, dt * T + i * 128:dt * T + (i + 1) * 128],
                                wsl, start=(dt == 0), stop=(dt == NDT - 1)))
                            if dt == NDT - 1:
                                mv = EI("t", emitfn, "MM", 1)
                            else:
                                E("t", emitfn)
                        psMW(pi, mv)
                    if i == 0:
                        EWL("t", wv_ready)
                    for nch in range(2):
                        pi = 2 + nch
                        psW(pi)
                        for dt in range(NDT):
                            wsl = hs[dt // 4][:, (dt % 4) * 1024 + nch * 512:
                                              (dt % 4) * 1024 + nch * 512 + 512]
                            emitfn = (lambda e, i=i, dt=dt, pi=pi, wsl=wsl: e.matmul(
                                ps[pi][:, :],
                                xnT[:, dt * T + i * 128:dt * T + (i + 1) * 128],
                                wsl, start=(dt == 0), stop=(dt == NDT - 1)))
                            if dt == NDT - 1:
                                mv = EI("t", emitfn, "MM", 1)
                            else:
                                E("t", emitfn)
                        psMW(pi, mv)
                    if i >= 2:
                        EW("s", "MM", ekv_rd[j])
                    psR(0, "s")
                    EI("s", lambda e, j=j: e.activation(
                        ek[j][:, 0:512], ps[0][:, :], AF.Exp, bias=negk0c[:, :]), "SS", 1)
                    psR(1, "s")
                    ek_ss[j] = EI("s", lambda e, j=j: e.activation(
                        ek[j][:, 512:1024], ps[1][:, :], AF.Exp, bias=negk0c[:, :]), "SS", 1)
                    psMR(0, "SS", ek_ss[j])
                    psMR(1, "SS", ek_ss[j])
                    if i >= 2:
                        EW("v", "MM", ekv_rd[j])
                    psR(2, "v")
                    EI("v", lambda e, j=j: e.tensor_copy(
                        vt[j][:, 0:4 * 129].rearrange("p (h e) -> p h e", e=129)[:, :, 0:128],
                        ps[2][:, :].rearrange("p (h e) -> p h e", e=128)), "VV", 1)
                    psR(3, "v")
                    vt_vv[j] = EI("v", lambda e, j=j: e.tensor_copy(
                        vt[j][:, 4 * 129:8 * 129].rearrange("p (h e) -> p h e", e=129)[:, :, 0:128],
                        ps[3][:, :].rearrange("p (h e) -> p h e", e=128)), "VV", 1)
                    psMR(2, "VV", vt_vv[j])
                    psMR(3, "VV", vt_vv[j])
                    EW("t", "SS", ek_ss[j])
                    EW("t", "VV", vt_vv[j])
                    for h in range(H):
                        pi = 5 + h // 3
                        co = (h % 3) * 129
                        emitfn = (lambda e, j=j, h=h, pi=pi, co=co, i=i: e.matmul(
                            ps[pi][:, co:co + 129],
                            ek[j][:, h * DH:(h + 1) * DH],
                            vt[j][:, h * 129:(h + 1) * 129],
                            start=(i == 0 and h % 3 == 0), stop=(i == NT - 1),
                            skip_group_check=True))
                        if h == H - 1:
                            ekv_rd[j] = EI("t", emitfn, "MM", 1)
                        else:
                            E("t", emitfn)
                s_acc_mm = cnt["MM"]
                for b in (5, 6, 7):
                    psMW(b, s_acc_mm)

                EW("s", "VV", spkg_rd_v)
                pk_ss = 0
                for h in range(H):
                    pi = 5 + h // 3
                    co = (h % 3) * 129
                    psR(pi, "s")
                    pk_ss = EI("s", lambda e, h=h, pi=pi, co=co: e.copy(
                        spkg[:, h * 129:(h + 1) * 129], ps[pi][:, co:co + 129]), "SS", 1)
                    psMR(pi, "SS", pk_ss)
                maybe_stop("spkgl", spkg[:, :], [("SS", pk_ss)])
                EW("g", "SS", pk_ss)
                dwi = EI("g", lambda e: e.dma_start(out=cc_in[:, :], in_=spkg[:, :]), "CD", 16)
                EW("g", "CD", dwi)
                ccv = EI("g", lambda e: e.collective_compute(
                    "AllReduce", ALU.add,
                    replica_groups=[[0, 1], [2, 3], [4, 5], [6, 7]],
                    ins=[cc_in.ap().opt()], outs=[cc_out.ap().opt()]), "CC", 1)
                wo_ready = load_mat_1024(hs[2:4], ("W2", "W3"), wo_d, l, wait_mm=s_acc_mm)
                EW("g", "CC", ccv)
                dwo = EI("g", lambda e: e.dma_start(out=spkg[:, :], in_=cc_out[:, :]), "CD", 16)

                EWL("t", wq_ready)
                eq_ss = 0
                for h in range(H):
                    for c in range(NCH):
                        pi = (h * NCH + c) % 4
                        psW(pi)
                        for dt in range(NDT):
                            wsl = hs4[:, (dt // 4) * 4096 + (dt % 4) * 1024 + h * 128:
                                      (dt // 4) * 4096 + (dt % 4) * 1024 + (h + 1) * 128]
                            emitfn = (lambda e, dt=dt, c=c, pi=pi, wsl=wsl, h=h: e.matmul(
                                ps[pi][:, :], wsl,
                                xnT[:, dt * T + c * 512:dt * T + c * 512 + 512],
                                start=(dt == 0), stop=(dt == NDT - 1)))
                            if dt == NDT - 1:
                                mv = EI("t", emitfn, "MM", 1)
                            else:
                                E("t", emitfn)
                        psMW(pi, mv)
                        psR(pi, "s")
                        eq_ss = EI("s", lambda e, h=h, c=c, pi=pi: e.activation(
                            eqT[:, h * T + c * 512:h * T + c * 512 + 512],
                            ps[pi][:, :], AF.Exp, bias=negk0c[:, :]), "SS", 1)
                        psMR(pi, "SS", eq_ss)
                q_done_mm = cnt["MM"]
                maybe_stop("eq", eqT[:, :], [("SS", eq_ss)])

                EW("v", "CD", dwo)
                ctx_vv = 0
                for h in range(H):
                    VW(lambda e, h=h: e.reciprocal(
                        rcol[0][:, :], spkg[:, h * 129 + 128:h * 129 + 129]))
                    ctx_vv = EI("v", lambda e, h=h: e.tensor_scalar_mul(
                        ctxbf[:, h * DH:(h + 1) * DH],
                        spkg[:, h * 129:h * 129 + 128], rcol[0][:, :]), "VV", 1)
                spkg_rd_v = ctx_vv
                maybe_stop("spkg", spkg[:, :], [("CD", dwo)])
                maybe_stop("ctx", ctxbf[:, :], [("VV", ctx_vv)])

                EW("t", "VV", ctx_vv)
                EW("t", "SS", eq_ss)
                attn_vv = 0
                attn_c_vv = [0] * NCH
                xa_wo_vv = 0

                wo_q = []
                wo_rot = [2, 3, 5]

                def emit_wo_tile(cw, ot, ridx):
                    nonlocal xa_wo_vv
                    pi = wo_rot[ridx % 3]
                    psW(pi)
                    if ot == 0:
                        EW("t", "VV", attn_c_vv[cw])
                    for h in range(H):
                        wsl = hs[2 + h // 4][:, (h % 4) * 1024 + ot * 128:
                                             (h % 4) * 1024 + (ot + 1) * 128]
                        emitfn = (lambda e, h=h, cw=cw, pi=pi, wsl=wsl: e.matmul(
                            ps[pi][:, :], wsl,
                            eqT[:, h * T + cw * 512:h * T + cw * 512 + 512],
                            start=(h == 0), stop=(h == H - 1)))
                        if h == H - 1:
                            mv2 = EI("t", emitfn, "MM", 1)
                        else:
                            E("t", emitfn)
                    psMW(pi, mv2)
                    psR(pi, "v")
                    boc = lncols[:, ot * 32 + 24 + l:ot * 32 + 24 + l + 1]
                    xa_wo_vv = EI("v", lambda e, ot=ot, cw=cw, pi=pi, boc=boc:
                                  e.scalar_tensor_tensor(
                                      xa[:, ot * T + cw * 512:ot * T + cw * 512 + 512],
                                      ps[pi][:, :], boc,
                                      xa[:, ot * T + cw * 512:ot * T + cw * 512 + 512],
                                      ALU.add, ALU.add), "VV", 1)
                    psMR(pi, "VV", xa_wo_vv)

                wo_ridx = 0
                for c in range(NCH):
                    for h in range(H):
                        pb = 6 + h % 2
                        psW(4)
                        mvz = EI("t", lambda e, h=h, c=c: e.matmul(
                            ps[4][0:1, :], onescol[:, :],
                            eqT[:, h * T + c * 512:h * T + c * 512 + 512],
                            start=True, stop=True), "MM", 1)
                        psMW(4, mvz)
                        psR(4, "s")
                        EI("s", lambda e: e.activation(arow[:, :], ps[4][0:1, :], AF.Ln), "SS", 1)
                        svr = EI("s", lambda e: e.activation(
                            mrow_b[:, :], arow[:, :], AF.Exp, scale=-1.0), "SS", 1)
                        psMR(4, "SS", svr)
                        if wo_q:
                            cw, ot = wo_q.pop(0)
                            emit_wo_tile(cw, ot, wo_ridx)
                            wo_ridx += 1
                        psW(pb)
                        EW("t", "SS", svr)
                        mvb = EI("t", lambda e, pb=pb: e.matmul(
                            ps[pb][:, :], onesrow[0:1, 0:128], mrow_b[:, :],
                            start=True, stop=True), "MM", 1)
                        psMW(pb, mvb)
                        pi = h % 2
                        psW(pi)
                        mvc = EI("t", lambda e, h=h, c=c, pi=pi: e.matmul(
                            ps[pi][:, :], ctxbf[:, h * DH:(h + 1) * DH],
                            eqT[:, h * T + c * 512:h * T + c * 512 + 512],
                            start=True, stop=True), "MM", 1)
                        psMW(pi, mvc)
                        psR(pi, "s")
                        EW("s", "VV", attn_vv)
                        sva = EI("s", lambda e, pi=pi: e.copy(aps2[:, :], ps[pi][:, :]), "SS", 1)
                        psMR(pi, "SS", sva)
                        EW("v", "SS", sva)
                        psR(pb, "v")
                        attn_vv = EI("v", lambda e, h=h, c=c, pb=pb: e.tensor_mul(
                            eqT[:, h * T + c * 512:h * T + c * 512 + 512],
                            aps2[:, :], ps[pb][:, :]), "VV", 1)
                        psMR(pb, "VV", attn_vv)
                    attn_c_vv[c] = attn_vv
                    wo_q.extend((c, ot) for ot in range(NDT))
                maybe_stop("attn", eqT[:, :], [("VV", attn_vv)])
                for cw, ot in wo_q:
                    emit_wo_tile(cw, ot, wo_ridx)
                    wo_ridx += 1
                wo_done_mm = cnt["MM"]
                xnT_free = s_acc_mm
                maybe_stop("wo", xa[:, :], [("VV", xa_wo_vv)])

                ln2_avv = emit_ln(l, 1, [("VV", xa_wo_vv), ("MM", q_done_mm)])
                maybe_stop("ln2", xnT[:, :], [("VV", ln2_avv[-1])])

                ffn_y_vv, ffn_h_mm, ffn_y_mm = emit_ffn(
                    l, ln2_avv, free_mm_hs01=s_acc_mm, free_mm_hs23=wo_done_mm)

                xa_ready = [("VV", ffn_y_vv)]
                xnT_free = ffn_h_mm
                if l + 1 < depth:
                    wq_ready = load_mat_1024(
                        [hs4[:, 0:4096], hs4[:, 4096:8192]], ("W4", "W5"),
                        wq_d, l + 1, wait_mm=q_done_mm)
                    wv_ready = load_mat_1024(hs[0:2], ("W0", "W1"), wv_d, l + 1,
                                             wait_mm=ffn_h_mm)
                    wk_ready = load_mat_1024(hs[2:4], ("W2", "W3"), wk_d, l + 1,
                                             wait_mm=ffn_y_mm)
                if ("L%d" % l) in dbg_t:
                    EW("y", "VV", ffn_y_vv)
                    dv = EI("y", lambda e, l=l: e.dma_start(
                        out=dbg_t["L%d" % l][:, :], in_=xa[:, :]), "OUT", 16)
                    dbg_done.append(dv)

            if "p0" in dbg_t:
                EW("y", "SS", c_ln0_ss)
                dbg_done.append(EI("y", lambda e: e.dma_start(
                    out=dbg_t["p0"][:, :], in_=xa[:, :]), "OUT", 16))
            if "wrt" in dbg_t:
                EW("y", "VV", c_router_vv)
                dbg_done.append(EI("y", lambda e: e.dma_start(
                    out=dbg_t["wrt"][:, :], in_=wrt[:, :]), "OUT", 16))

            if do_moe:
                EW("y", "MM", xnT_free)
                xn0_rdy = EI("y", lambda e: e.dma_start(
                    out=xnT[:, :], in_=xn0_dram[:, :]), "F0", 16)

                spl_dx = 0
                for i in range(NT):
                    j = i % 2
                    for dt in range(NDT):
                        pi = dt % 4
                        psW(pi)
                        if i == 0 and dt == 0:
                            for s_, v_ in xa_ready:
                                EW("t", s_, v_)
                        mv = EI("t", lambda e, i=i, dt=dt, pi=pi: e.transpose(
                            ps[pi][:, 0:128],
                            xa[:, dt * T + i * 128:dt * T + (i + 1) * 128],
                            identf[:, :]), "MM", 1)
                        psMW(pi, mv)
                        psR(pi, "s")
                        if dt == 0 and i >= 2:
                            EW("s", "SP", spl_dx)
                        sv = EI("s", lambda e, j=j, dt=dt, pi=pi: e.copy(
                            stb[j][:, dt * 128:(dt + 1) * 128], ps[pi][:, 0:128]), "SS", 1)
                        psMR(pi, "SS", sv)
                    EW("y", "SS", cnt["SS"])
                    spl_dx = EI("y", lambda e, i=i, j=j: e.dma_start(
                        out=s_dram[i * 128:(i + 1) * 128, :], in_=stb[j]), "SP", 16)
                s_spill_mm = cnt["MM"]
                s_spill_dx = spl_dx


                NBLK = NFT // 4
                h_mm = [0] * NBLK
                y_mm_s = [0] * (NBLK // 2)
                w1_rdy = [0] * NBLK
                w2_rdy = [0] * NBLK
                b2_ss_mm = 0
                first_y_guard = [("MM", s_spill_mm), ("SP", s_spill_dx)]

                for ex in range(NE):
                    EW("g", "MM", b2_ss_mm)
                    b2_rdy = EI("g", lambda e, ex=ex: e.dma_start(
                        out=b2row[:, :], in_=b2m_d[ex:ex + 1, :]), "B2", 16)

                    def load_w1_m(B, wmm, ex=ex):
                        if wmm:
                            EW("g", "MM", wmm)
                        srcw = W1m_d[ex, :, B * 512:(B + 1) * 512].rearrange(
                            "(kt p) m -> p kt m", p=128)
                        dstw = hs[B % 2][:, :].rearrange("p (kt m) -> p kt m", m=512)
                        EI("g", lambda e, d=dstw,
                           s=srcw: e.dma_start(out=d, in_=s), "W%d" % (B % 2), 16)
                        w1_rdy[B] = ("W%d" % (B % 2), cnt["W%d" % (B % 2)])

                    def load_w2_m(B, wmm2, ex=ex):
                        if wmm2:
                            EW("g", "MM", wmm2)
                        for fl in range(4):
                            ft = B * 4 + fl
                            src = W2m_d[ex, ft * 128:(ft + 1) * 128, :]
                            EI("g", lambda e, d=hs[2 + B % 2][:, fl * 1024:(fl + 1) * 1024],
                               s=src: e.dma_start(out=d, in_=s), "W%d" % (2 + B % 2), 16)
                        w2_rdy[B] = ("W%d" % (2 + B % 2), cnt["W%d" % (2 + B % 2)])

                    init_w1a = (ffn_h_mm if (depth > 0 and ex == 0) else
                                (h_mm[NBLK - 2] if ex > 0 else 0))
                    init_w1b = (ffn_h_mm if (depth > 0 and ex == 0) else
                                (h_mm[NBLK - 1] if ex > 0 else 0))
                    init_w2 = (ffn_y_mm if (depth > 0 and ex == 0) else
                               (y_mm_s[-1] if ex > 0 else 0))
                    load_w1_m(0, init_w1a)
                    load_w2_m(0, init_w2)
                    load_w1_m(1, init_w1b)
                    load_w2_m(1, init_w2)
                    relu_ss = {}
                    for S in range(NBLK // 2):
                        for B in (2 * S, 2 * S + 1):
                            EW("t", *w1_rdy[B])
                            if ex == 0 and B == 0:
                                EW("t", "F0", xn0_rdy)
                            half = (B % 2) * (H * T // 2)
                            for fl in range(4):
                                ft = B * 4 + fl
                                for c in range(NCH):
                                    pi = 4 + (fl * NCH + c) % 2
                                    psW(pi)
                                    for dt in range(NDT):
                                        emitfn = (lambda e, fl=fl, c=c, dt=dt, pi=pi, B=B:
                                          e.matmul(
                                            ps[pi][:, :],
                                            hs[B % 2][:, dt * 512 + fl * 128:
                                                      dt * 512 + (fl + 1) * 128],
                                            xnT[:, dt * T + c * 512:dt * T + c * 512 + 512],
                                            start=(dt == 0), stop=(dt == NDT - 1)))
                                        if dt == NDT - 1:
                                            mv = EI("t", emitfn, "MM", 1)
                                        else:
                                            E("t", emitfn)
                                    psMW(pi, mv)
                                    psR(pi, "s")
                                    if ex == 0 and B == 0 and fl == 0 and c == 0:
                                        EW("s", "SP", s_spill_dx)
                                    bcol = b1c[:, ft * NE + ex:ft * NE + ex + 1]
                                    sv = EI("s", lambda e, half=half, fl=fl, c=c, pi=pi,
                                            bcol=bcol: e.activation(
                                                eqT[:, half + fl * T + c * 512:
                                                    half + fl * T + c * 512 + 512],
                                                ps[pi][:, :], AF.Relu, bias=bcol), "SS", 1)
                                    psMR(pi, "SS", sv)
                                    relu_ss[(B, fl, c)] = sv
                            h_mm[B] = cnt["MM"]
                            if B + 2 < NBLK:
                                load_w1_m(B + 2, h_mm[B])
                        EW("t", *w2_rdy[2 * S])
                        EW("t", *w2_rdy[2 * S + 1])
                        if S == 0:
                            EW("t", "B2", b2_rdy)
                        EW("t", "SS", relu_ss[(2 * S + 1, 3, NCH - 1)])
                        for tt in range(NT):
                            for oc in range(2):
                                pi = (tt * 2 + oc) % 4
                                psW(pi)
                                n = 0
                                for B in (2 * S, 2 * S + 1):
                                    half = (B % 2) * (H * T // 2)
                                    for fl in range(4):
                                        emitfn = (lambda e, B=B, fl=fl, tt=tt, oc=oc, pi=pi,
                                            half=half, n=n, S=S: e.matmul(
                                            ps[pi][:, :],
                                            eqT[:, half + fl * T + tt * 128:
                                                half + fl * T + (tt + 1) * 128],
                                            hs[2 + B % 2][:, fl * 1024 + oc * 512:
                                                          fl * 1024 + oc * 512 + 512],
                                            start=(n == 0), stop=(n == 7 and S != 0)))
                                        if n == 7 and S != 0:
                                            mv = EI("t", emitfn, "MM", 1)
                                        else:
                                            E("t", emitfn)
                                        n += 1
                                if S == 0:
                                    mv = EI("t", lambda e, pi=pi, oc=oc: e.matmul(
                                        ps[pi][:, :], onesrow[0:1, 0:128],
                                        b2row[0:1, oc * 512:oc * 512 + 512],
                                        start=False, stop=True), "MM", 1)
                                psMW(pi, mv)
                                psR(pi, "v")
                                if ex == 0 and S == 0 and tt == 0 and oc == 0:
                                    for s_, v_ in first_y_guard:
                                        EW("v", s_, v_)
                                wcol = wrt[:, tt * NE + ex:tt * NE + ex + 1]
                                sl = slice(tt * D + oc * 512, tt * D + oc * 512 + 512)
                                if ex == 0 and S == 0:
                                    yv = EI("v", lambda e, sl=sl, pi=pi, wcol=wcol:
                                            e.tensor_scalar_mul(
                                                xa[:, sl], ps[pi][:, :], wcol), "VV", 1)
                                else:
                                    yv = EI("v", lambda e, sl=sl, pi=pi, wcol=wcol:
                                            e.scalar_tensor_tensor(
                                                xa[:, sl], ps[pi][:, :], wcol,
                                                xa[:, sl], ALU.mult, ALU.add), "VV", 1)
                                psMR(pi, "VV", yv)
                        y_mm_s[S] = cnt["MM"]
                        for B2 in (2 * S + 2, 2 * S + 3):
                            if B2 < NBLK:
                                load_w2_m(B2, y_mm_s[S])
                    b2_ss_mm = y_mm_s[0]
                acc_vv = cnt["VV"]

                ld_x0 = [0, 0]
                ld_st = [0, 0]
                used = [0, 0]
                EW("y", "MM", cnt["MM"])
                EW("y", "SP", s_spill_dx)
                for i in range(2):
                    ld_x0[i] = EI("y", lambda e, i=i: e.dma_start(
                        out=x0b[i], in_=x_in[i * 128:(i + 1) * 128, :]), "F%d" % i, 16)
                    ld_st[i] = EI("y", lambda e, i=i: e.dma_start(
                        out=stb[i], in_=s_dram[i * 128:(i + 1) * 128, :]), "F%d" % i, 16)
                out_dx = 0
                prev_uvv = 0
                for i in range(NT):
                    j = i % 2
                    EW("s", "VV", max(acc_vv, prev_uvv))
                    sv = EI("s", lambda e, i=i: e.activation(
                        gbuf, xa[:, i * D:(i + 1) * D], AF.Silu), "SS", 1)
                    EW("v", "SS", sv)
                    EW("v", "F%d" % j, max(ld_x0[j], ld_st[j]))
                    EI("v", lambda e, j=j: e.tensor_mul(
                        gbuf, gbuf, stb[j]), "VV", 1)
                    uvv = EI("v", lambda e, i=i, j=j: e.tensor_add(
                        xa[:, i * D:(i + 1) * D], gbuf, x0b[j]), "VV", 1)
                    used[j] = uvv
                    prev_uvv = uvv
                    EW("y", "VV", uvv)
                    out_dx = EI("y", lambda e, i=i: e.dma_start(
                        out=out_d[i * 128:(i + 1) * 128, :],
                        in_=xa[:, i * D:(i + 1) * D]), "OUT", 16)
                    if i + 2 < NT:
                        EW("y", "VV", used[j])
                        ld_x0[j] = EI("y", lambda e, i=i, j=j: e.dma_start(
                            out=x0b[j], in_=x_in[(i + 2) * 128:(i + 3) * 128, :]),
                            "F%d" % j, 16)
                        ld_st[j] = EI("y", lambda e, i=i, j=j: e.dma_start(
                            out=stb[j], in_=s_dram[(i + 2) * 128:(i + 3) * 128, :]),
                            "F%d" % j, 16)

        except StopEmit:
            pass

        EW("y", "OUT", cnt["OUT"])
        EW("y", "CC", cnt["CC"])

        block = ctx.enter_context(nc.Block())

        @block.tensor
        def _(e):
            for f in streams["t"]:
                f(e)

        @block.vector
        def _(e):
            for f in streams["v"]:
                f(e)

        @block.scalar
        def _(e):
            for f in streams["s"]:
                f(e)

        @block.gpsimd
        def _(e):
            for f in streams["g"]:
                f(e)

        @block.sync
        def _(e):
            for f in streams["y"]:
                f(e)

    return nc


def make_in_maps(inputs, depth=6):
    f = np.ascontiguousarray
    x = np.asarray(inputs["L0_attn_x"], np.float32)
    com = {
        "wq": f(np.asarray(inputs["L0_attn_wq"], np.float32)),
        "wk": f(np.asarray(inputs["L0_attn_wk"], np.float32)),
        "wv": f(np.asarray(inputs["L0_attn_wv"], np.float32)),
        "wo": f(np.asarray(inputs["L0_attn_wo"], np.float32)),
        "w1": f(np.asarray(inputs["L0_attn_w1"], np.float32)),
        "w2": f(np.asarray(inputs["L0_attn_w2"], np.float32)),
        "ln1w": f(np.asarray(inputs["L0_attn_ln1_w"], np.float32)),
        "ln1b": f(np.asarray(inputs["L0_attn_ln1_b"], np.float32)),
        "ln2w": f(np.asarray(inputs["L0_attn_ln2_w"], np.float32)),
        "ln2b": f(np.asarray(inputs["L0_attn_ln2_b"], np.float32)),
        "bo": f(np.asarray(inputs["L0_attn_bo"], np.float32)),
        "wr": f(np.asarray(inputs["L0_ffn_W_router"], np.float32)),
        "br": f(np.asarray(inputs["L0_ffn_b_router"], np.float32).reshape(1, NE)),
        "W1m": f(np.asarray(inputs["L0_ffn_W1"], np.float32)),
        "b1m": f(np.asarray(inputs["L0_ffn_b1"], np.float32)),
        "W2m": f(np.asarray(inputs["L0_ffn_W2"], np.float32)),
        "b2m": f(np.asarray(inputs["L0_ffn_b2"], np.float32)),
        "ident": np.eye(128, dtype=np.float32),
    }
    maps = []
    for c in range(8):
        b, hh = c // 2, c % 2
        m = dict(com)
        m["x_in"] = f(x[b, hh * T:(hh + 1) * T, :])
        maps.append(m)
    return maps


def kernel(**inputs):
    nc = build_program(depth=6, do_moe=True)
    maps = make_in_maps(inputs)
    res = run_bass_kernel_spmd(nc, maps, core_ids=list(range(8)))
    out = np.empty((4, 4096, 1024), np.float32)
    for c in range(8):
        b, hh = c // 2, c % 2
        out[b, hh * T:(hh + 1) * T, :] = np.asarray(res.results[c]["out"])
    return out
